# version 04
# baseline (speedup 1.0000x reference)
import math
import contextlib

import numpy as np
import ml_dtypes

import concourse.bass as bass
import concourse.mybir as mybir
from concourse.bass_utils import run_bass_kernel_spmd

F32 = mybir.dt.float32
BF16 = mybir.dt.bfloat16
AF = mybir.ActivationFunctionType
ALU = mybir.AluOpType
AX = mybir.AxisListType

MAGIC = 12582912.0
PI = math.pi
EPS = 1e-5

T = 2048
NT = 16
D = 1024
DC = 8
HP = 8
DI = 4096
NE = 8
HE = 512
BFNP = ml_dtypes.bfloat16


class Plan:
    ENGS = ("t", "v", "s", "g", "y")

    def __init__(self):
        self.q = {e: [] for e in self.ENGS}
        self.n = {nm: 0 for nm in ("T", "V", "S", "G", "Y", "GD", "CC")}

    def w(self, eng, sem, val=None):
        v = self.n[sem] if val is None else val
        if v > 0:
            self.q[eng].append(("w", sem, v))

    def r(self, eng, fn, inc=None, k=1):
        if inc is not None:
            self.n[inc] += k
        self.q[eng].append(("r", fn, inc, k))
        return self.n[inc] if inc is not None else None


def build_nc(stop_after=None):
    nc = bass.Bass("TRN2", target_bir_lowering=False)
    nc.detect_race_conditions = False

    dt = nc.dram_tensor
    x_in = dt("x", [T, D], F32, kind="ExternalInput").ap()
    w_in = {}
    for nm, sh in [("wq", [D, D]), ("wk", [D, D]), ("wv", [D, D]),
                   ("wo", [D, D]), ("w1", [D, DI]), ("w2", [DI, D]),
                   ("proj", [64, 128]), ("gate", [D, NE]),
                   ("mw1", [NE * D, HE]), ("mw3", [NE * D, HE]),
                   ("mw2", [NE * HE, D])]:
        w_in[nm] = dt(nm, sh, F32, kind="ExternalInput").ap()
    cols_in = dt("colsin", [128, 88], F32, kind="ExternalInput").ap()
    rows_in = dt("rowsin", [1, 2048], F32, kind="ExternalInput").ap()
    eyeB_in = dt("eyeB", [128, 128], BF16, kind="ExternalInput").ap()
    ones_in = dt("ones1", [1, 128], BF16, kind="ExternalInput").ap()

    y_out = dt("y", [T, D], F32, kind="ExternalOutput").ap()
    dbg = {}
    if stop_after is not None:
        for nm, sh, ddt in [("dbg_fma", [128, 16384], BF16),
                            ("dbg_fmb", [128, 16384], BF16),
                            ("dbg_str", [128, 16384], F32),
                            ("dbg_kvb", [128, 2048], BF16),
                            ("dbg_kvp", [128, 2048], BF16),
                            ("dbg_cols", [128, 128], F32),
                            ("dbg_ss", [128, 128], F32),
                            ("dbg_scb", [128, 6144], BF16),
                            ("dbg_scr", [128, 3136], F32),
                            ("dbg_wa", [128, 8192], BF16),
                            ("dbg_wb", [128, 8192], BF16)]:
            dbg[nm] = dt(nm, sh, ddt, kind="ExternalOutput").ap()

    cc_in_h = dt("cc_in", [128, 2048], BF16)
    cc_out_h = dt("cc_out", [128, 2048], BF16)

    stk = contextlib.ExitStack()
    sb = lambda n, s, d: stk.enter_context(nc.sbuf_tensor(n, s, d))
    psal = lambda n, s, d: stk.enter_context(nc.psum_tensor(n, s, d))

    STR = sb("STR", [128, NT * D], F32)
    FMA = sb("FMA", [128, 16384], BF16)
    FMB = sb("FMB", [128, 16384], BF16)
    WA = sb("WA", [128, 8192], BF16)
    WB = sb("WB", [128, 8192], BF16)
    WL = sb("WL", [128, 2048], F32)
    SCR = sb("SCR", [128, 3136], F32)
    SCB = sb("SCB", [128, 6144], BF16)
    KVB = sb("KVB", [128, 2048], BF16)
    KVP = sb("KVP", [128, 2048], BF16)
    KVA = sb("KVA", [128, 256], F32)
    COLS = sb("COLS", [128, 128], F32)
    ROWS = sb("ROWSb", [1, 2048], BF16)
    EYEB = sb("EYEB", [128, 128], BF16)
    ONE1 = sb("ONE1", [1, 128], BF16)
    PRJ = sb("PRJ", [128, 128], BF16)
    GTB = sb("GTB", [128, 64], BF16)
    SS = sb("SSb", [128, 128], F32)
    STT = sb("STTc", [128, 16], F32)

    OU, OV, OR_, OA = 0, 512, 1024, 1536
    OASC = 0
    ODMP = 2048
    OG1, OG2, OG3 = 3072, 3080, 3088
    OATT = 0
    OKP, OVFM, OVP = 0, 1024, 1536
    OH1, OA3, OHS, OHF = 0, 512, 1024, 1536
    OQPC, OQPS = 4096, 4608
    OQA, OQB = 5120, 5632
    OXN = 5120
    CG1, CL1B, CG2, CL2B = 0, 8, 16, 24
    CB1 = 32
    CBQ, CBK, CBV = 64, 72, 80
    CB1P = 88
    (CSU0, CSU1, CSQ0, CSQ1, CMEAN, CMSQE, CVARP, CSD, CRSTD, CNMUR,
     CZERO, CPIO2, CG_A, CG_B, CG_C, CG_D) = range(16)

    pA = psal("pA", [128, 512], F32)
    pB = psal("pB", [128, 512], F32)
    pC = psal("pC", [128, 512], F32)
    pD = psal("pD", [128, 512], F32)
    pE = psal("pE", [128, 512], F32)
    pF = psal("pF", [128, 512], F32)
    pT = psal("pT", [128, 1024], BF16)
    pX = psal("pX", [128, 512], F32)

    sems = {nm: stk.enter_context(nc.semaphore("sem" + nm))
            for nm in ("T", "V", "S", "G", "Y", "GD", "CC")}

    P = Plan()
    zero = STT[:, CZERO:CZERO + 1]

    def mm(out, lhsT, rhs, start, stop, tile_position=None, inc=None):
        def f(e, o=out, l=lhsT, r=rhs, st=start, sp=stop, tp=tile_position):
            return e.matmul(out=o, lhsT=l, rhs=r, start=st, stop=sp,
                            skip_group_check=True, tile_position=tp)
        P.r("t", f, inc=inc)

    def tpose(out, in_, ident, inc=None):
        P.r("t", lambda e, o=out, i=in_, d=ident:
            e.transpose(out=o, in_=i, identity=d), inc=inc)

    def act(out, in_, func, bias, scale=1.0, accum=None):
        P.r("s", lambda e, o=out, i=in_, f=func, b=bias, sc=scale, a=accum:
            e.activation(out=o, in_=i, func=f, bias=b, scale=sc, accum_out=a),
            inc="S")

    def vts(out, in0, s1, op0, s2=None, op1=None, eng="v"):
        inc = {"v": "V", "g": "G"}[eng]
        if op1 is None:
            P.r(eng, lambda e, o=out, i=in0, a=s1, p0=op0:
                e.tensor_scalar(out=o, in0=i, scalar1=a, scalar2=None, op0=p0),
                inc=inc)
        else:
            P.r(eng, lambda e, o=out, i=in0, a=s1, b=s2, p0=op0, p1=op1:
                e.tensor_scalar(out=o, in0=i, scalar1=a, scalar2=b,
                                op0=p0, op1=p1), inc=inc)

    def vtt(out, in0, in1, op):
        P.r("v", lambda e, o=out, a=in0, b=in1, p=op:
            e.tensor_tensor(out=o, in0=a, in1=b, op=p), inc="V")

    def vstt(out, in0, scalar, in1, op0, op1):
        P.r("v", lambda e, o=out, a=in0, sc=scalar, b=in1, p0=op0, p1=op1:
            e.scalar_tensor_tensor(out=o, in0=a, scalar=sc, in1=b,
                                   op0=p0, op1=p1), inc="V")

    def vcopy(out, in_, eng="v"):
        P.r(eng, lambda e, o=out, i=in_: e.tensor_copy(out=o, in_=i),
            inc={"v": "V", "g": "G"}[eng])

    def vred(out, in_, op):
        P.r("v", lambda e, o=out, i=in_, p=op:
            e.tensor_reduce(out=o, in_=i, axis=AX.X, op=p), inc="V")

    def vrecip(out, in_):
        P.r("v", lambda e, o=out, i=in_: e.reciprocal(out=o, in_=i), inc="V")

    def dma(eng, out, in_):
        P.r(eng, lambda e, o=out, i=in_: e.dma_start(out=o, in_=i),
            inc={"y": "Y", "g": "GD"}[eng], k=16)

    def memset(ap, val, eng="v"):
        P.r(eng, lambda e, a=ap, v=val: e.memset(a, v),
            inc={"v": "V", "g": "G"}[eng])

    def fma_tile_view(buf, t):
        return buf[:, :].rearrange("p (c r) -> p c r", r=2048)[:, :, t * 128:(t + 1) * 128]

    def stt_c(c):
        return STT[:, c:c + 1]

    def vsettle():
        P.r("v", lambda e: e.tensor_copy(out=SCR[:, 3104:3136],
                                         in_=SCR[:, 3104:3136]), inc="V")

    def ssettle():
        P.r("s", lambda e: e.activation(out=SCR[:, 3104:3136],
                                        in_=SCR[:, 3104:3136],
                                        func=AF.Identity, bias=0.0,
                                        scale=1.0), inc="S")

    def ln_tile(ins, outs, dump_off=ODMP):
        n = len(ins)
        for i, (ap, wd) in enumerate(ins):
            vtt(SCR[:, dump_off:dump_off + wd], ap, ap, ALU.mult)
            vred(stt_c(CSQ0 + i), SCR[:, dump_off:dump_off + wd], ALU.add)
            vred(stt_c(CSU0 + i), ap, ALU.add)
        vsettle()
        vsettle()
        if n == 2:
            vtt(stt_c(CSU0), stt_c(CSU0), stt_c(CSU1), ALU.add)
            vtt(stt_c(CSQ0), stt_c(CSQ0), stt_c(CSQ1), ALU.add)
        vts(stt_c(CMEAN), stt_c(CSU0), 1.0 / D, ALU.mult)
        vsettle()
        vts(stt_c(CMSQE), stt_c(CMEAN), stt_c(CMEAN), ALU.mult,
            -EPS, ALU.subtract)
        vsettle()
        vts(stt_c(CVARP), stt_c(CSQ0), 1.0 / D, ALU.mult,
            stt_c(CMSQE), ALU.subtract)
        vsettle()
        vsettle()
        P.w("s", "V")
        act(stt_c(CSD), stt_c(CVARP), AF.Sqrt, zero)
        ssettle()
        P.w("v", "S")
        vrecip(stt_c(CRSTD), stt_c(CSD))
        vsettle()
        vts(stt_c(CNMUR), stt_c(CMEAN), stt_c(CRSTD), ALU.mult, -1.0, ALU.mult)
        vsettle()
        vsettle()
        P.w("s", "V")
        for (ap, wd), oap in zip(ins, outs):
            act(oap, ap, AF.Identity, stt_c(CNMUR), scale=stt_c(CRSTD))

    wl_state = {"half": 0, "mark": [[], []]}

    def land(src_rear3, nchunks, ncols):
        h = wl_state["half"]
        for sem, val in wl_state["mark"][h]:
            P.w("y", sem, val)
        wl_state["mark"][h] = []
        wl = WL[:, h * 1024:h * 1024 + nchunks * ncols]
        dma("y", wl.rearrange("p (c f) -> p c f", c=nchunks), src_rear3)
        wl_state["half"] = 1 - h
        return h, P.n["Y"]

    def wl_mark(h, sem):
        wl_state["mark"][h].append((sem, P.n[sem]))

    def wrear(w_ap):
        return w_ap.rearrange("(c p) f -> p c f", p=128)

    slot_mark = {}

    def wait_slot(eng, key):
        if key in slot_mark:
            P.w(eng, "T", slot_mark[key])

    def set_slot(key):
        slot_mark[key] = P.n["T"]

    def load_w_dd(w_ap, dstbuf, dstkey, scale_coff=None, const_scale=None,
                  bias_rhs_coff=None, bias_dst_coff=None):
        for ci in range(DC):
            h, ydone = land(wrear(w_ap)[:, ci:ci + 1, :], 1, 1024)
            seg = WL[:, h * 1024:(h + 1) * 1024]
            od = dstbuf[:, ci * 1024:(ci + 1) * 1024]
            P.w("g", "Y", ydone)
            if ci == 0:
                wait_slot("g", dstkey)
            if scale_coff is not None:
                vts(od, seg, COLS[:, scale_coff + ci:scale_coff + ci + 1],
                    ALU.mult, eng="g")
            elif const_scale is not None:
                vts(od, seg, const_scale, ALU.mult, eng="g")
            else:
                vcopy(od, seg, eng="g")
            if bias_rhs_coff is not None:
                P.w("t", "Y", ydone)
                pp = pX if ci % 2 == 0 else pE
                if ci >= 2:
                    P.w("t", "V")
                for oc in range(8):
                    mm(pp[:, oc * 8:(oc + 1) * 8],
                       seg[:, oc * 128:(oc + 1) * 128],
                       COLS[:, bias_rhs_coff:bias_rhs_coff + 8],
                       start=True, stop=True, inc=("T" if oc == 7 else None))
                wl_mark(h, "T")
                wl_mark(h, "G")
                P.w("v", "T")
                vstt(COLS[:, bias_dst_coff:bias_dst_coff + 8],
                     COLS[:, bias_dst_coff:bias_dst_coff + 8], 1.0,
                     pp[:, 0:64].rearrange("p (o j) -> p o j", j=8)[:, :, ci],
                     ALU.mult, ALU.add)
            else:
                wl_mark(h, "G")

    def emit_dumps(names):
        allb = {"dbg_fma": FMA, "dbg_fmb": FMB, "dbg_str": STR,
                "dbg_kvb": KVB, "dbg_kvp": KVP, "dbg_cols": COLS,
                "dbg_ss": SS, "dbg_scb": SCB, "dbg_scr": SCR,
                "dbg_wa": WA, "dbg_wb": WB}
        for eng, semn in [("y", "T"), ("y", "V"), ("y", "S"), ("y", "G")]:
            P.w(eng, semn)
        for nm in names:
            if nm == "dbg_cols":
                dma("y", dbg[nm][:, 0:120], COLS[:, 0:120])
            else:
                dma("y", dbg[nm], allb[nm][:, :])
        P.w("y", "Y")

    if stop_after is not None:
        memset(SCB[:, :], 0.0)
        memset(SCR[:, :], 0.0)
        memset(SS[:, :], 0.0)
        P.w("y", "V")
    dma("y", EYEB[:, :], eyeB_in)
    dma("y", ONE1[0:1, :], ones_in)
    dma("y", SCR[0:64, 0:128], w_in["proj"])
    dma("y", SCR[64:128, 0:128], w_in["proj"])
    dma("y", COLS[:, 0:88], cols_in)
    dma("y", SCR[0:1, 512:2560], rows_in)
    for c in range(DC):
        dma("y", SCR[:, 128 + c * 8:128 + (c + 1) * 8],
            w_in["gate"][c * 128:(c + 1) * 128, :])
    dma("y", STR[:, :].rearrange("p (t f) -> p t f", t=NT),
        x_in.rearrange("(t p) f -> p t f", p=128))
    setup_y = P.n["Y"]

    memset(SCR[:, 3104:3136], 0.0)
    P.w("v", "Y", setup_y)
    memset(COLS[:, CB1P:CB1P + 32], 0.0)
    memset(stt_c(CZERO), 0.0)
    memset(stt_c(CPIO2), PI / 2)
    vcopy(PRJ[0:64, :], SCR[0:64, 0:128])
    vcopy(PRJ[64:128, :], SCR[64:128, 0:128])
    vcopy(GTB[:, :], SCR[:, 128:192])
    vcopy(ROWS[0:1, :], SCR[0:1, 512:2560])

    def barrier_all():
        for eng in ("t", "v", "s", "g", "y"):
            for semn in ("T", "V", "S", "G", "Y", "GD"):
                P.w(eng, semn)

    P.w("s", "Y", setup_y)
    xnb = SCB[:, OXN:OXN + 1024]
    for t in range(NT):
        xt = STR[:, t * D:(t + 1) * D]
        P.w("s", "T")
        ln_tile([(xt, 1024)], [xnb])
        ssettle()
        P.w("t", "S")
        P.w("t", "V")
        for c in range(DC):
            tpose(pT[:, c * 128:(c + 1) * 128], xnb[:, c * 128:(c + 1) * 128],
                  EYEB[:, :], inc=("T" if c == DC - 1 else None))
        P.w("v", "T")
        vcopy(fma_tile_view(FMA, t),
              pT[:, 0:1024].rearrange("p (c r) -> p c r", c=DC))

    if stop_after == "A":
        emit_dumps(["dbg_fma", "dbg_str"])
        return _finish(nc, stk, P, sems)

    load_w_dd(w_in["wk"], WA, "WA", scale_coff=CG1,
              bias_rhs_coff=CL1B, bias_dst_coff=CBK)
    load_w_dd(w_in["wv"], WB, "WB", scale_coff=CG1,
              bias_rhs_coff=CL1B, bias_dst_coff=CBV)
    wkv_cast = P.n["G"]
    wkv_bias = P.n["V"]
    if stop_after == "W":
        P.w("y", "G")
        emit_dumps(["dbg_cols", "dbg_wa", "dbg_wb"])
        return _finish(nc, stk, P, sems)

    KP = SCB[:, OKP:OKP + 1024]
    KPv = KP.rearrange("p (c r) -> p c r", r=256)
    vfm = SCB[:, OVFM:OVFM + 512]
    VPb = SCB[:, OVP:OVP + 512]
    kAb = SCB[:, OQA:OQA + 512]
    kBb = SCB[:, OQB:OQB + 512]

    def fourier(src_psum, sink_cos, sink_sin):
        u = SCR[:, OU:OU + 512]
        v = SCR[:, OV:OV + 512]
        r = SCR[:, OR_:OR_ + 512]
        a = SCR[:, OA:OA + 512]
        P.w("v", "S")
        vts(u, src_psum, 1.0 / (16 * PI), ALU.mult, MAGIC, ALU.add)
        vts(v, u, MAGIC, ALU.subtract, -2 * PI, ALU.mult)
        vstt(r, src_psum, 0.125, v, ALU.mult, ALU.add)
        vsettle()
        P.w("s", "V")
        P.w("s", "T")
        act(a, r, AF.Abs, zero)
        act(sink_cos, a, AF.Sin, stt_c(CPIO2), scale=-1.0)
        act(sink_sin, r, AF.Sin, zero)
        ssettle()

    P.w("t", "G", wkv_cast)
    P.w("t", "V", wkv_bias)
    for pr in range(HP):
        memset(KVA[:, :], 0.0)
        for tc in range(4):
            P.w("t", "S")
            for c in range(DC):
                mm(pA[:, :], WA[:, c * 1024 + pr * 128:c * 1024 + (pr + 1) * 128],
                   FMA[:, c * 2048 + tc * 512:c * 2048 + (tc + 1) * 512],
                   start=(c == 0), stop=(c == DC - 1),
                   inc=("T" if c == DC - 1 else None))
            P.w("s", "T")
            act(kAb[0:64, :], pA[0:64, :], AF.Identity,
                COLS[0:64, CBK + pr:CBK + pr + 1])
            act(kBb[64:128, :], pA[64:128, :], AF.Identity,
                COLS[64:128, CBK + pr:CBK + pr + 1])
            ssettle()
            k_evac = P.n["S"]
            for c in range(DC):
                mm(pC[:, :], WB[:, c * 1024 + pr * 128:c * 1024 + (pr + 1) * 128],
                   FMA[:, c * 2048 + tc * 512:c * 2048 + (tc + 1) * 512],
                   start=(c == 0), stop=(c == DC - 1),
                   inc=("T" if c == DC - 1 else None))
            P.w("s", "T")
            act(vfm[:, :], pC[:, :], AF.Identity,
                COLS[:, CBV + pr:CBV + pr + 1])
            ssettle()
            v_evac = P.n["S"]
            P.w("t", "S", v_evac)
            P.w("t", "V")
            for ttc in range(4):
                tpose(pT[:, ttc * 128:(ttc + 1) * 128],
                      vfm[:, ttc * 128:(ttc + 1) * 128], EYEB[:, :],
                      inc=("T" if ttc == 3 else None))
            P.w("v", "T")
            vcopy(VPb[:, :], pT[:, 0:512])
            vsettle()
            vp_ready = P.n["V"]
            for hh in range(2):
                P.w("t", "S", k_evac)
                P.w("t", "V")
                for ttc in range(4):
                    if hh == 0:
                        mm(pB[:, ttc * 128:(ttc + 1) * 128],
                           kAb[0:64, ttc * 128:(ttc + 1) * 128],
                           PRJ[0:64, :], start=(ttc == 0), stop=(ttc == 3),
                           inc=("T" if ttc == 3 else None))
                    else:
                        mm(pB[:, ttc * 128:(ttc + 1) * 128],
                           kBb[64:128, ttc * 128:(ttc + 1) * 128],
                           PRJ[64:128, :], start=(ttc == 0), stop=(ttc == 3),
                           inc=("T" if ttc == 3 else None))
                P.w("v", "T")
                fourier(pB[:, :], KPv[:, :, 0:128], KPv[:, :, 128:256])
                kp_ready = P.n["S"]
                P.w("t", "S", kp_ready)
                P.w("t", "V", vp_ready)
                for ttc in range(4):
                    lh = VPb[:, ttc * 128 + hh * 64:ttc * 128 + hh * 64 + 64]
                    if hh == 0:
                        mm(pD[0:64, 0:256], lh, KP[:, ttc * 256:(ttc + 1) * 256],
                           start=(ttc == 0), stop=(ttc == 3),
                           inc=("T" if ttc == 3 else None))
                    else:
                        mm(pD[64:128, 0:256], lh, KP[:, ttc * 256:(ttc + 1) * 256],
                           start=(ttc == 0), stop=(ttc == 3),
                           tile_position=(0, 64),
                           inc=("T" if ttc == 3 else None))
            P.w("v", "T")
            vtt(KVA[:, :], pD[:, 0:256], KVA[:, :], ALU.add)
            P.w("t", "V")
        vsettle()
        vts(KVP[:, pr * 256:(pr + 1) * 256], KVA[:, :], 1.0, ALU.mult)
        if stop_after == "B0" and pr == 0:
            emit_dumps(["dbg_kvp", "dbg_scb", "dbg_scr"])
            return _finish(nc, stk, P, sems)
    kvp_done = P.n["V"]
    set_slot("WA")
    set_slot("WB")
    if stop_after == "B1":
        emit_dumps(["dbg_kvp", "dbg_fma"])
        return _finish(nc, stk, P, sems)

    P.w("g", "V", kvp_done)
    dma("g", cc_in_h.ap(), KVP[:, :])
    P.w("g", "GD")
    P.r("g", lambda e: e.collective_compute(
        "AllReduce", ALU.add,
        replica_groups=[[0, 1, 2, 3], [4, 5, 6, 7]],
        ins=[cc_in_h.ap().opt()], outs=[cc_out_h.ap().opt()]), inc="CC")

    load_w_dd(w_in["wq"], WA, "WA", scale_coff=CG1,
              bias_rhs_coff=CL1B, bias_dst_coff=CBQ)
    load_w_dd(w_in["wo"], WB, "WB", const_scale=1.0 / 128.0)
    wqo_cast = P.n["G"]
    wq_bias = P.n["V"]

    P.w("y", "CC")
    dma("y", KVP[0:64, :], cc_out_h.ap()[0:64, :])
    dma("y", SCB[0:64, 0:2048], cc_out_h.ap()[64:128, :])
    kvsum_y = P.n["Y"]

    P.w("t", "Y", kvsum_y)
    for half in range(4):
        P.w("t", "V")
        for i in range(8):
            gidx = half * 8 + i
            h = gidx // 2
            mc = gidx % 2
            pr = h // 2
            hh = h % 2
            buf = KVP if hh == 0 else SCB
            src = buf[0:64,
                      pr * 256 + mc * 128:pr * 256 + (mc + 1) * 128]
            idn = EYEB[0:64, 0:64]
            tpose(pT[:, i * 64:(i + 1) * 64], src, idn,
                  inc=("T" if i == 7 else None))
        P.w("v", "T")
        vcopy(KVB[:, half * 512:(half + 1) * 512], pT[:, 0:512])
    kvb_done = P.n["V"]

    if stop_after == "B":
        emit_dumps(["dbg_fma", "dbg_str", "dbg_kvb", "dbg_kvp", "dbg_cols"])
        return _finish(nc, stk, P, sems)

    ATT = SCB[:, OATT:OATT + 4096]
    QPC = SCB[:, OQPC:OQPC + 512]
    QPS = SCB[:, OQPS:OQPS + 512]
    qAb = SCB[:, OQA:OQA + 512]
    qBb = SCB[:, OQB:OQB + 512]
    xn2 = SCB[:, OXN:OXN + 1024]

    P.w("t", "G", wqo_cast)
    P.w("t", "V", max(wq_bias, kvb_done))
    for tc in range(4):
        P.w("v", "S")
        for pr in range(HP):
            P.w("t", "S")
            for c in range(DC):
                mm(pA[:, :], WA[:, c * 1024 + pr * 128:c * 1024 + (pr + 1) * 128],
                   FMA[:, c * 2048 + tc * 512:c * 2048 + (tc + 1) * 512],
                   start=(c == 0), stop=(c == DC - 1),
                   inc=("T" if c == DC - 1 else None))
            P.w("s", "T")
            act(qAb[0:64, :], pA[0:64, :], AF.Identity,
                COLS[0:64, CBQ + pr:CBQ + pr + 1])
            act(qBb[64:128, :], pA[64:128, :], AF.Identity,
                COLS[64:128, CBQ + pr:CBQ + pr + 1])
            ssettle()
            q_evac = P.n["S"]
            for hh in range(2):
                hgl = 2 * pr + hh
                P.w("t", "S", q_evac)
                P.w("t", "V")
                if hh == 0:
                    mm(pB[:, :], PRJ[0:64, :], qAb[0:64, :],
                       start=True, stop=True, inc="T")
                else:
                    mm(pB[:, :], PRJ[64:128, :], qBb[64:128, :],
                       start=True, stop=True, inc="T")
                P.w("v", "T")
                if stop_after == "C1" and tc == 0 and pr == 0 and hh == 0:
                    vcopy(SCR[:, 2048:2560], pB[:, :])
                    vcopy(SCR[:, 2560:2688], PRJ[:, :])
                    vcopy(SCR[:, 2688:2816], pA[:, 0:128])
                    emit_dumps(["dbg_scb", "dbg_str", "dbg_fma", "dbg_scr", "dbg_wa", "dbg_wb"])
                    return _finish(nc, stk, P, sems)
                fourier(pB[:, :], QPC, QPS)
                qp_ready = P.n["S"]
                P.w("t", "S", qp_ready)
                if hh == 0:
                    mm(pC[0:64, :], KVB[:, hgl * 128:hgl * 128 + 64], QPC,
                       start=True, stop=False)
                    mm(pC[0:64, :], KVB[:, hgl * 128 + 64:hgl * 128 + 128], QPS,
                       start=False, stop=True, inc="T")
                else:
                    mm(pC[64:128, :], KVB[:, hgl * 128:hgl * 128 + 64], QPC,
                       start=True, stop=False, tile_position=(0, 64))
                    mm(pC[64:128, :], KVB[:, hgl * 128 + 64:hgl * 128 + 128], QPS,
                       start=False, stop=True, tile_position=(0, 64), inc="T")
            P.w("s", "T")
            act(ATT[:, pr * 512:(pr + 1) * 512], pC[:, :], AF.Identity, zero)
            ssettle()
        att_ready = P.n["S"]
        for tsub in range(4):
            t = tc * 4 + tsub
            P.w("t", "S", att_ready)
            P.w("t", "V")
            for nh in range(2):
                pp = pD if nh == 0 else pE
                mm(pp[:, :], ONE1[0:1, :], ROWS[0:1, nh * 512:(nh + 1) * 512],
                   start=True, stop=False)
                for pr in range(HP):
                    mm(pp[:, :],
                       ATT[:, pr * 512 + tsub * 128:pr * 512 + (tsub + 1) * 128],
                       WB[:, pr * 1024 + nh * 512:pr * 1024 + (nh + 1) * 512],
                       start=False, stop=(pr == HP - 1),
                       inc=("T" if pr == HP - 1 else None))
            P.w("v", "T")
            Asc = SCR[:, OASC:OASC + 1024]
            vtt(Asc[:, 0:512], pD[:, :], STR[:, t * D:t * D + 512], ALU.add)
            vtt(Asc[:, 512:1024], pE[:, :], STR[:, t * D + 512:(t + 1) * D],
                ALU.add)
            vtt(STR[:, t * D:(t + 1) * D], Asc[:, :],
                STR[:, t * D:(t + 1) * D], ALU.add)
            P.w("s", "V")
            P.w("s", "T")
            ln_tile([(Asc, 1024)], [xn2])
            ssettle()
            P.w("t", "S")
            P.w("t", "V")
            for c in range(DC):
                tpose(pT[:, c * 128:(c + 1) * 128],
                      xn2[:, c * 128:(c + 1) * 128], EYEB[:, :],
                      inc=("T" if c == DC - 1 else None))
            P.w("v", "T")
            vcopy(fma_tile_view(FMB, t),
                  pT[:, 0:1024].rearrange("p (c r) -> p c r", c=DC))

    set_slot("WA")
    set_slot("WB")

    if stop_after == "C":
        emit_dumps(["dbg_fma", "dbg_fmb", "dbg_str", "dbg_kvb", "dbg_cols", "dbg_scb"])
        return _finish(nc, stk, P, sems)

    P.w("v", "T")
    vcopy(COLS[:, CB1P:CB1P + 32], COLS[:, CB1:CB1 + 32])
    HB = FMA
    for tq in range(4):
        for hs in range(DC):
            slot = hs % 2
            wslot = WA[:, slot * 4096:(slot + 1) * 4096]
            for lnd in range(4):
                h, ydone = land(
                    wrear(w_in["w1"])[:, lnd * 2:lnd * 2 + 2,
                                      hs * 512:(hs + 1) * 512], 2, 512)
                P.w("g", "Y", ydone)
                if lnd == 0:
                    wait_slot("g", ("W1", slot))
                    wait_slot("g", "WA")
                for j in range(2):
                    ci = lnd * 2 + j
                    seg = WL[:, h * 1024 + j * 512:h * 1024 + (j + 1) * 512]
                    vts(wslot[:, ci * 512:(ci + 1) * 512], seg,
                        COLS[:, CG2 + ci:CG2 + ci + 1], ALU.mult, eng="g")
                if tq == 0:
                    P.w("t", "Y", ydone)
                    for j in range(2):
                        ci = lnd * 2 + j
                        seg = WL[:, h * 1024 + j * 512:h * 1024 + (j + 1) * 512]
                        pp = pX if ci % 2 == 0 else pC
                        if ci >= 2:
                            P.w("t", "V")
                        for hc in range(4):
                            mm(pp[:, hc * 8:(hc + 1) * 8],
                               seg[:, hc * 128:(hc + 1) * 128],
                               COLS[:, CL2B:CL2B + 8],
                               start=True, stop=True,
                               inc=("T" if hc == 3 else None))
                        P.w("v", "T")
                        vstt(COLS[:, CB1P + hs * 4:CB1P + hs * 4 + 4],
                             COLS[:, CB1P + hs * 4:CB1P + hs * 4 + 4], 1.0,
                             pp[:, 0:32].rearrange(
                                 "p (o j) -> p o j", j=8)[:, :, ci],
                             ALU.mult, ALU.add)
                    wl_mark(h, "T")
                    wl_mark(h, "G")
                else:
                    wl_mark(h, "G")
            w1_ready = P.n["G"]
            b1_ready = P.n["V"]
            P.w("t", "G", w1_ready)
            P.w("t", "V", b1_ready)
            for hc in range(4):
                hcg = hs * 4 + hc
                pp = pA if hcg % 2 == 0 else pB
                P.w("t", "S")
                for c in range(DC):
                    mm(pp[:, :],
                       wslot[:, c * 512 + hc * 128:c * 512 + (hc + 1) * 128],
                       FMB[:, c * 2048 + tq * 512:c * 2048 + (tq + 1) * 512],
                       start=(c == 0), stop=(c == DC - 1),
                       inc=("T" if c == DC - 1 else None))
                P.w("s", "T")
                act(HB[:, hcg * 512:(hcg + 1) * 512], pp[:, :], AF.Relu,
                    COLS[:, CB1P + hcg:CB1P + hcg + 1])
            set_slot(("W1", slot))
        ssettle()
        h_done_s = P.n["S"]
        for nh in range(2):
            P.w("t", "V")
            for hs in range(DC):
                slot = hs % 2
                w2slot = WB[:, slot * 2048:(slot + 1) * 2048]
                for lnd in range(2):
                    h, ydone = land(
                        wrear(w_in["w2"])[:, hs * 4 + lnd * 2:hs * 4 + lnd * 2 + 2,
                                          nh * 512:(nh + 1) * 512], 2, 512)
                    P.w("g", "Y", ydone)
                    if lnd == 0:
                        wait_slot("g", ("W2", slot))
                        wait_slot("g", "WB")
                    for j in range(2):
                        hsub = lnd * 2 + j
                        seg = WL[:, h * 1024 + j * 512:h * 1024 + (j + 1) * 512]
                        vcopy(w2slot[:, hsub * 512:(hsub + 1) * 512], seg,
                              eng="g")
                    wl_mark(h, "G")
                w2_ready = P.n["G"]
                P.w("t", "G", w2_ready)
                if hs == 0:
                    P.w("t", "S", h_done_s)
                for tsub in range(4):
                    pp = [pC, pD, pE, pF][tsub]
                    if hs == 0:
                        mm(pp[:, :], ONE1[0:1, :],
                           ROWS[0:1, 1024 + nh * 512:1024 + (nh + 1) * 512],
                           start=True, stop=False)
                    for hsub in range(4):
                        hcg = hs * 4 + hsub
                        mm(pp[:, :],
                           HB[:, hcg * 512 + tsub * 128:hcg * 512 + (tsub + 1) * 128],
                           w2slot[:, hsub * 512:(hsub + 1) * 512],
                           start=False, stop=(hs == DC - 1 and hsub == 3),
                           inc=("T" if hsub == 3 else None))
                set_slot(("W2", slot))
            for tsub in range(4):
                t = tq * 4 + tsub
                pp = [pC, pD, pE, pF][tsub]
                P.w("v", "T")
                if nh == 0:
                    vtt(SCR[:, tsub * 512:(tsub + 1) * 512], pp[:, :],
                        STR[:, t * D:t * D + 512], ALU.add)
                else:
                    vtt(SCR[:, ODMP:ODMP + 512], pp[:, :],
                        STR[:, t * D + 512:(t + 1) * D], ALU.add)
                    P.w("s", "V")
                    ln_tile([(SCR[:, tsub * 512:(tsub + 1) * 512], 512),
                             (SCR[:, ODMP:ODMP + 512], 512)],
                            [STR[:, t * D:t * D + 512],
                             STR[:, t * D + 512:(t + 1) * D]],
                            dump_off=ODMP + 512)
                    P.w("v", "S")

    set_slot("FMB")

    if stop_after == "D":
        emit_dumps(["dbg_str", "dbg_cols"])
        return _finish(nc, stk, P, sems)

    ybs = SCB[:, OXN:OXN + 1024]
    for t in range(NT):
        P.w("v", "S")
        P.w("v", "T")
        vts(ybs[:, :], STR[:, t * D:(t + 1) * D], 1.0, ALU.mult)
        P.w("t", "V")
        for c in range(DC):
            tpose(pT[:, c * 128:(c + 1) * 128], ybs[:, c * 128:(c + 1) * 128],
                  EYEB[:, :], inc=("T" if c == DC - 1 else None))
        P.w("v", "T")
        vcopy(fma_tile_view(FMA, t),
              pT[:, 0:1024].rearrange("p (c r) -> p c r", c=DC))
    vsettle()
    y1t_done = P.n["V"]

    P.w("t", "V", y1t_done)
    for tt in range(NT):
        P.w("t", "S")
        for c in range(DC):
            mm(pX[:, 0:8], FMA[:, c * 2048 + tt * 128:c * 2048 + (tt + 1) * 128],
               GTB[:, c * 8:(c + 1) * 8], start=(c == 0), stop=(c == DC - 1),
               inc=("T" if c == DC - 1 else None))
        P.w("v", "T")
        ssl = SS[:, tt * 8:(tt + 1) * 8]
        vred(stt_c(CG_A), pX[:, 0:8], ALU.max)
        vsettle()
        vts(stt_c(CG_B), stt_c(CG_A), -1.0, ALU.mult)
        vsettle()
        P.w("s", "V")
        act(SCR[:, OG1:OG1 + 8], pX[:, 0:8], AF.Exp, stt_c(CG_B))
        ssettle()
        P.w("v", "S")
        vred(stt_c(CG_A), SCR[:, OG1:OG1 + 8], ALU.add)
        vsettle()
        vrecip(stt_c(CG_B), stt_c(CG_A))
        vsettle()
        vts(ssl, SCR[:, OG1:OG1 + 8], stt_c(CG_B), ALU.mult)
        vsettle()
        vred(stt_c(CG_C), ssl, ALU.max)
        vsettle()
        vts(SCR[:, OG2:OG2 + 8], ssl, stt_c(CG_C), ALU.is_ge)
        vsettle()
        vstt(SCR[:, OG3:OG3 + 8], SCR[:, OG2:OG2 + 8], -1e30, ssl,
             ALU.mult, ALU.add)
        vsettle()
        vred(stt_c(CG_D), SCR[:, OG3:OG3 + 8], ALU.max)
        vsettle()
        vts(SCR[:, OG2:OG2 + 8], ssl, stt_c(CG_D), ALU.is_ge)
        vsettle()
        vtt(ssl, ssl, SCR[:, OG2:OG2 + 8], ALU.mult)
    gate_done = P.n["V"]

    h1b = SCB[:, OH1:OH1 + 512]
    a3b = SCB[:, OA3:OA3 + 512]
    hsb = SCB[:, OHS:OHS + 512]
    hfb = SCB[:, OHF:OHF + 512]
    P.w("s", "V", gate_done)
    for grp in range(4):
        gp = grp % 2
        for ei in range(2):
            e = grp * 2 + ei
            for src_nm, dbuf, base, key in [
                    ("mw1", WA, ei * 4096, ("EW1", ei)),
                    ("mw3", WB, ei * 4096, ("EW3", ei))]:
                wsl = w_in[src_nm][e * D:(e + 1) * D, :]
                for lnd in range(4):
                    h, ydone = land(wrear(wsl)[:, lnd * 2:lnd * 2 + 2, :], 2, 512)
                    P.w("g", "Y", ydone)
                    if lnd == 0:
                        wait_slot("g", key)
                        if src_nm == "mw1":
                            wait_slot("g", ("W1", ei))
                        else:
                            wait_slot("g", ("W2", 0))
                            wait_slot("g", ("W2", 1))
                            wait_slot("g", "WB")
                    for j in range(2):
                        ci = lnd * 2 + j
                        seg = WL[:, h * 1024 + j * 512:h * 1024 + (j + 1) * 512]
                        vcopy(dbuf[:, base + ci * 512:base + (ci + 1) * 512],
                              seg, eng="g")
                    wl_mark(h, "G")
            w2sl = w_in["mw2"][e * HE:(e + 1) * HE, :]
            fbase = gp * 8192 + ei * 4096
            for lnd in range(4):
                h, ydone = land(wrear(w2sl)[:, lnd:lnd + 1, :], 1, 1024)
                P.w("g", "Y", ydone)
                if lnd == 0:
                    wait_slot("g", ("EW2", gp, ei))
                    wait_slot("g", "FMB")
                seg = WL[:, h * 1024:(h + 1) * 1024]
                vcopy(FMB[:, fbase + lnd * 1024:fbase + (lnd + 1) * 1024],
                      seg, eng="g")
                wl_mark(h, "G")
        ew_cast = P.n["G"]
        P.w("t", "G", ew_cast)
        for tq in range(8):
            P.w("t", "V")
            for ei in range(2):
                e = grp * 2 + ei
                for tsub in range(2):
                    tt = tq * 2 + tsub
                    toff = tq * 256 + tsub * 128
                    P.w("t", "S")
                    for c in range(DC):
                        mm(pA[:, :], FMA[:, c * 2048 + toff:c * 2048 + toff + 128],
                           WA[:, ei * 4096 + c * 512:ei * 4096 + (c + 1) * 512],
                           start=(c == 0), stop=(c == DC - 1),
                           inc=("T" if c == DC - 1 else None))
                    for c in range(DC):
                        mm(pB[:, :], FMA[:, c * 2048 + toff:c * 2048 + toff + 128],
                           WB[:, ei * 4096 + c * 512:ei * 4096 + (c + 1) * 512],
                           start=(c == 0), stop=(c == DC - 1),
                           inc=("T" if c == DC - 1 else None))
                    P.w("s", "T")
                    P.w("s", "V")
                    sse = SS[:, tt * 8 + e:tt * 8 + e + 1]
                    act(h1b, pA[:, :], AF.Silu, zero, scale=sse)
                    P.w("v", "S")
                    P.w("v", "T")
                    vts(a3b, pB[:, :], sse, ALU.mult)
                    vtt(hsb, h1b, a3b, ALU.mult)
                    vsettle()
                    P.w("t", "V")
                    for i4 in range(4):
                        tpose(pT[:, i4 * 128:(i4 + 1) * 128],
                              hsb[:, i4 * 128:(i4 + 1) * 128], EYEB[:, :],
                              inc=("T" if i4 == 3 else None))
                    P.w("v", "T")
                    vcopy(hfb, pT[:, 0:512])
                    vsettle()
                    hf_ready = P.n["V"]
                    P.w("t", "V", hf_ready)
                    for nh in range(2):
                        pp = [pC, pD, pE, pF][tsub * 2 + nh]
                        for hc in range(4):
                            w2b = gp * 8192 + ei * 4096 + hc * 1024 + nh * 512
                            mm(pp[:, :], hfb[:, hc * 128:(hc + 1) * 128],
                               FMB[:, w2b:w2b + 512],
                               start=(ei == 0 and hc == 0),
                               stop=(ei == 1 and hc == 3),
                               inc=("T" if hc == 3 else None))
            for tsub in range(2):
                tt = tq * 2 + tsub
                for nh in range(2):
                    pp = [pC, pD, pE, pF][tsub * 2 + nh]
                    P.w("v", "T")
                    vtt(STR[:, tt * D + nh * 512:tt * D + (nh + 1) * 512],
                        pp[:, :],
                        STR[:, tt * D + nh * 512:tt * D + (nh + 1) * 512],
                        ALU.add)
        for ei in range(2):
            set_slot(("EW1", ei))
            set_slot(("EW3", ei))
            set_slot(("EW2", gp, ei))
    moe_done = P.n["V"]

    out_mark = [0, 0]
    P.w("s", "V", moe_done)
    for t in range(NT):
        png = t % 2
        if out_mark[png] > 0:
            P.w("s", "Y", out_mark[png])
        ln_tile([(STR[:, t * D:(t + 1) * D], 1024)],
                [SCR[:, png * 1024:(png + 1) * 1024]])
        ssettle()
        s_done = P.n["S"]
        P.w("y", "S", s_done)
        dma("y", y_out[t * 128:(t + 1) * 128, :],
            SCR[:, png * 1024:(png + 1) * 1024])
        out_mark[png] = P.n["Y"]

    return _finish(nc, stk, P, sems)


def _finish(nc, stk, P, sems):
    P.w("y", "Y")
    block = stk.enter_context(nc.Block())

    def replay(engname):
        def run(e):
            for step in P.q[engname]:
                if step[0] == "w":
                    e.wait_ge(sems[step[1]], step[2])
                else:
                    instr = step[1](e)
                    if step[2] is not None:
                        instr.then_inc(sems[step[2]], step[3])
        return run

    block.tensor(replay("t"))
    block.vector(replay("v"))
    block.scalar(replay("s"))
    block.gpsimd(replay("g"))
    block.sync(replay("y"))
    stk.close()
    return nc


def make_inputs(inputs):
    f32 = lambda a: np.ascontiguousarray(np.asarray(a), dtype=np.float32)
    cols = np.concatenate([
        f32(inputs["L0_attn_ln1_g"]).reshape(8, 128).T,
        f32(inputs["L0_attn_ln1_b"]).reshape(8, 128).T,
        f32(inputs["L0_attn_ln2_g"]).reshape(8, 128).T,
        f32(inputs["L0_attn_ln2_b"]).reshape(8, 128).T,
        f32(inputs["L0_attn_b1"]).reshape(32, 128).T,
        f32(inputs["L0_attn_bq"]).reshape(8, 128).T,
        f32(inputs["L0_attn_bk"]).reshape(8, 128).T,
        f32(inputs["L0_attn_bv"]).reshape(8, 128).T,
    ], axis=1)
    rows = np.concatenate([
        f32(inputs["L0_attn_bo"]).reshape(1, 1024),
        f32(inputs["L0_attn_b2"]).reshape(1, 1024),
    ], axis=1)
    base = {
        "wq": f32(inputs["L0_attn_wq"]),
        "wk": f32(inputs["L0_attn_wk"]),
        "wv": f32(inputs["L0_attn_wv"]),
        "wo": f32(inputs["L0_attn_wo"]),
        "w1": f32(inputs["L0_attn_w1"]),
        "w2": f32(inputs["L0_attn_w2"]),
        "proj": f32(inputs["L0_attn_proj_mat"]),
        "gate": f32(inputs["L0_ffn_gate_w"]),
        "mw1": f32(inputs["L0_ffn_w1"]).reshape(NE * D, HE),
        "mw3": f32(inputs["L0_ffn_w3"]).reshape(NE * D, HE),
        "mw2": f32(inputs["L0_ffn_w2"]).reshape(NE * HE, D),
        "colsin": np.ascontiguousarray(cols),
        "rowsin": np.ascontiguousarray(rows),
        "eyeB": np.eye(128, dtype=BFNP),
        "ones1": np.ones((1, 128), dtype=BFNP),
    }
    x = np.asarray(inputs["L0_attn_x"], dtype=np.float32).reshape(16384, D)
    shards = [np.ascontiguousarray(x[i * T:(i + 1) * T]) for i in range(8)]
    return base, shards


_NC_CACHE = {}


def kernel(**inputs):
    if "nc" not in _NC_CACHE:
        _NC_CACHE["nc"] = build_nc()
    nc = _NC_CACHE["nc"]
    base, shards = make_inputs(inputs)
    in_maps = [dict(base, x=shards[i]) for i in range(8)]
    res = run_bass_kernel_spmd(nc, in_maps, core_ids=list(range(8)))
    out = np.concatenate([np.asarray(res.results[c]["y"]) for c in range(8)],
                         axis=0)
    return out.reshape(2, 8192, 1024)


# version 08
# speedup vs baseline: 1.4995x; 1.4995x over previous
import math
import contextlib

import numpy as np
import ml_dtypes

import concourse.bass as bass
import concourse.mybir as mybir
from concourse.bass_utils import run_bass_kernel_spmd

F32 = mybir.dt.float32
BF16 = mybir.dt.bfloat16
AF = mybir.ActivationFunctionType
ALU = mybir.AluOpType
AX = mybir.AxisListType

MAGIC = 12582912.0
PI = math.pi
EPS = 1e-5

T = 2048
NT = 16
D = 1024
DC = 8
HP = 8
DI = 4096
NE = 8
HE = 512
BFNP = ml_dtypes.bfloat16


class Plan:
    ENGS = ("t", "v", "s", "g", "y")

    def __init__(self):
        self.q = {e: [] for e in self.ENGS}
        self.n = {nm: 0 for nm in ("T", "V", "S", "G", "Y", "GD", "CC",
                                   "Y0", "Y1", "YS", "YK", "YO0", "YO1")}

    def w(self, eng, sem, val=None):
        v = self.n[sem] if val is None else val
        if v > 0:
            self.q[eng].append(("w", sem, v))

    def r(self, eng, fn, inc=None, k=1):
        if inc is not None:
            self.n[inc] += k
        self.q[eng].append(("r", fn, inc, k))
        return self.n[inc] if inc is not None else None


def build_nc(stop_after=None):
    nc = bass.Bass("TRN2", target_bir_lowering=False)
    nc.detect_race_conditions = False

    dt = nc.dram_tensor
    x_in = dt("x", [T, D], F32, kind="ExternalInput").ap()
    w_in = {}
    for nm, sh in [("wq", [D, D]), ("wk", [D, D]), ("wv", [D, D]),
                   ("wo", [D, D]), ("w1", [D, DI]), ("w2", [DI, D]),
                   ("proj", [64, 128]), ("gate", [D, NE]),
                   ("mw1", [NE * D, HE]), ("mw3", [NE * D, HE]),
                   ("mw2", [NE * HE, D])]:
        w_in[nm] = dt(nm, sh, F32, kind="ExternalInput").ap()
    cols_in = dt("colsin", [128, 88], F32, kind="ExternalInput").ap()
    rows_in = dt("rowsin", [1, 2048], F32, kind="ExternalInput").ap()
    eyeB_in = dt("eyeB", [128, 128], BF16, kind="ExternalInput").ap()
    ones_in = dt("ones1", [1, 128], BF16, kind="ExternalInput").ap()

    y_out = dt("y", [T, D], F32, kind="ExternalOutput").ap()
    dbg = {}
    if stop_after is not None:
        for nm, sh, ddt in [("dbg_fma", [128, 16384], BF16),
                            ("dbg_fmb", [128, 16384], BF16),
                            ("dbg_str", [128, 16384], F32),
                            ("dbg_kvb", [128, 2048], BF16),
                            ("dbg_kvp", [128, 2048], BF16),
                            ("dbg_cols", [128, 128], F32),
                            ("dbg_ss", [128, 128], F32),
                            ("dbg_scb", [128, 6144], BF16),
                            ("dbg_scr", [128, 3136], F32),
                            ("dbg_wa", [128, 8192], BF16),
                            ("dbg_wb", [128, 8192], BF16)]:
            dbg[nm] = dt(nm, sh, ddt, kind="ExternalOutput").ap()

    cc_in_h = dt("cc_in", [128, 2048], BF16)
    cc_out_h = dt("cc_out", [128, 2048], BF16)

    stk = contextlib.ExitStack()
    sb = lambda n, s, d: stk.enter_context(nc.sbuf_tensor(n, s, d))
    psal = lambda n, s, d: stk.enter_context(nc.psum_tensor(n, s, d))

    STR = sb("STR", [128, NT * D], F32)
    FMA = sb("FMA", [128, 16384], BF16)
    FMB = sb("FMB", [128, 16384], BF16)
    WA = sb("WA", [128, 8192], BF16)
    WB = sb("WB", [128, 8192], BF16)
    WL = sb("WL", [128, 2048], F32)
    SCR = sb("SCR", [128, 3136], F32)
    SCB = sb("SCB", [128, 6144], BF16)
    KVB = sb("KVB", [128, 2048], BF16)
    KVP = sb("KVP", [128, 2048], BF16)
    KVA = sb("KVA", [128, 256], F32)
    COLS = sb("COLS", [128, 128], F32)
    ROWS = sb("ROWSb", [1, 2048], BF16)
    EYEB = sb("EYEB", [128, 128], BF16)
    ONE1 = sb("ONE1", [1, 128], BF16)
    PRJ = sb("PRJ", [128, 128], BF16)
    GTB = sb("GTB", [128, 64], BF16)
    SS = sb("SSb", [128, 128], F32)
    STT = sb("STTc", [128, 16], F32)

    OU, OV, OR_, OA = 0, 512, 1024, 1536
    OASC = 0
    ODMP = 2048
    OG1, OG2, OG3 = 3072, 3080, 3088
    OATT = 0
    OKP, OVFM, OVP = 0, 1024, 1536
    OH1, OA3, OHS, OHF = 0, 512, 1024, 1536
    OQPC, OQPS = 4096, 4608
    OQA, OQB = 5120, 5632
    OXN = 5120
    CG1, CL1B, CG2, CL2B = 0, 8, 16, 24
    CB1 = 32
    CBQ, CBK, CBV = 64, 72, 80
    CB1P = 88
    (CSU0, CSU1, CSQ0, CSQ1, CMEAN, CMSQE, CVARP, CSD, CRSTD, CNMUR,
     CZERO, CPIO2, CG_A, CG_B, CG_C, CG_D) = range(16)

    pA = psal("pA", [128, 512], F32)
    pB = psal("pB", [128, 512], F32)
    pC = psal("pC", [128, 512], F32)
    pD = psal("pD", [128, 512], F32)
    pE = psal("pE", [128, 512], F32)
    pF = psal("pF", [128, 512], F32)
    pT = psal("pT", [128, 1024], BF16)
    pX = psal("pX", [128, 512], F32)

    sems = {nm: stk.enter_context(nc.semaphore("sem" + nm))
            for nm in ("T", "V", "S", "G", "Y", "GD", "CC",
                       "Y0", "Y1", "YS", "YK", "YO0", "YO1")}

    P = Plan()
    zero = STT[:, CZERO:CZERO + 1]

    def mm(out, lhsT, rhs, start, stop, tile_position=None, inc=None):
        def f(e, o=out, l=lhsT, r=rhs, st=start, sp=stop, tp=tile_position):
            return e.matmul(out=o, lhsT=l, rhs=r, start=st, stop=sp,
                            skip_group_check=True, tile_position=tp)
        P.r("t", f, inc=inc)

    def tpose(out, in_, ident, inc=None):
        P.r("t", lambda e, o=out, i=in_, d=ident:
            e.transpose(out=o, in_=i, identity=d), inc=inc)

    def act(out, in_, func, bias, scale=1.0, accum=None):
        P.r("s", lambda e, o=out, i=in_, f=func, b=bias, sc=scale, a=accum:
            e.activation(out=o, in_=i, func=f, bias=b, scale=sc, accum_out=a),
            inc="S")

    def vts(out, in0, s1, op0, s2=None, op1=None, eng="v"):
        inc = {"v": "V", "g": "G"}[eng]
        if op1 is None:
            P.r(eng, lambda e, o=out, i=in0, a=s1, p0=op0:
                e.tensor_scalar(out=o, in0=i, scalar1=a, scalar2=None, op0=p0),
                inc=inc)
        else:
            P.r(eng, lambda e, o=out, i=in0, a=s1, b=s2, p0=op0, p1=op1:
                e.tensor_scalar(out=o, in0=i, scalar1=a, scalar2=b,
                                op0=p0, op1=p1), inc=inc)

    def vtt(out, in0, in1, op):
        P.r("v", lambda e, o=out, a=in0, b=in1, p=op:
            e.tensor_tensor(out=o, in0=a, in1=b, op=p), inc="V")

    def vstt(out, in0, scalar, in1, op0, op1):
        P.r("v", lambda e, o=out, a=in0, sc=scalar, b=in1, p0=op0, p1=op1:
            e.scalar_tensor_tensor(out=o, in0=a, scalar=sc, in1=b,
                                   op0=p0, op1=p1), inc="V")

    def vcopy(out, in_, eng="v"):
        P.r(eng, lambda e, o=out, i=in_: e.tensor_copy(out=o, in_=i),
            inc={"v": "V", "g": "G"}[eng])

    def vred(out, in_, op):
        P.r("v", lambda e, o=out, i=in_, p=op:
            e.tensor_reduce(out=o, in_=i, axis=AX.X, op=p), inc="V")

    def vrecip(out, in_):
        P.r("v", lambda e, o=out, i=in_: e.reciprocal(out=o, in_=i), inc="V")

    def dma(eng, out, in_, sem=None):
        P.r(eng, lambda e, o=out, i=in_: e.dma_start(out=o, in_=i),
            inc=(sem if sem is not None else {"y": "Y", "g": "GD"}[eng]),
            k=16)

    def memset(ap, val, eng="v"):
        P.r(eng, lambda e, a=ap, v=val: e.memset(a, v),
            inc={"v": "V", "g": "G"}[eng])

    def fma_tile_view(buf, t):
        return buf[:, :].rearrange("p (c r) -> p c r", r=2048)[:, :, t * 128:(t + 1) * 128]

    def stt_c(c):
        return STT[:, c:c + 1]

    def vsettle():
        P.r("v", lambda e: e.tensor_copy(out=SCR[:, 3104:3136],
                                         in_=SCR[:, 3104:3136]), inc="V")

    def ssettle():
        P.r("s", lambda e: e.activation(out=SCR[:, 3104:3136],
                                        in_=SCR[:, 3104:3136],
                                        func=AF.Identity, bias=0.0,
                                        scale=1.0), inc="S")

    def ln_tile(ins, outs, dump_off=ODMP):
        n = len(ins)
        for i, (ap, wd) in enumerate(ins):
            vtt(SCR[:, dump_off:dump_off + wd], ap, ap, ALU.mult)
            vred(stt_c(CSQ0 + i), SCR[:, dump_off:dump_off + wd], ALU.add)
            vred(stt_c(CSU0 + i), ap, ALU.add)
        vsettle()
        vsettle()
        if n == 2:
            vtt(stt_c(CSU0), stt_c(CSU0), stt_c(CSU1), ALU.add)
            vtt(stt_c(CSQ0), stt_c(CSQ0), stt_c(CSQ1), ALU.add)
        vts(stt_c(CMEAN), stt_c(CSU0), 1.0 / D, ALU.mult)
        vsettle()
        vts(stt_c(CMSQE), stt_c(CMEAN), stt_c(CMEAN), ALU.mult,
            -EPS, ALU.subtract)
        vsettle()
        vts(stt_c(CVARP), stt_c(CSQ0), 1.0 / D, ALU.mult,
            stt_c(CMSQE), ALU.subtract)
        vsettle()
        vsettle()
        P.w("s", "V")
        act(stt_c(CSD), stt_c(CVARP), AF.Sqrt, zero)
        ssettle()
        P.w("v", "S")
        vrecip(stt_c(CRSTD), stt_c(CSD))
        vsettle()
        vts(stt_c(CNMUR), stt_c(CMEAN), stt_c(CRSTD), ALU.mult, -1.0, ALU.mult)
        vsettle()
        vsettle()
        P.w("s", "V")
        for (ap, wd), oap in zip(ins, outs):
            act(oap, ap, AF.Identity, stt_c(CNMUR), scale=stt_c(CRSTD))

    wl_state = {"half": 0, "mark": [[], []]}

    def land(src_rear3, nchunks, ncols):
        h = wl_state["half"]
        for sem, val in wl_state["mark"][h]:
            P.w("y", sem, val)
        wl_state["mark"][h] = []
        wl = WL[:, h * 1024:h * 1024 + nchunks * ncols]
        dma("y", wl.rearrange("p (c f) -> p c f", c=nchunks), src_rear3,
            sem="Y%d" % h)
        wl_state["half"] = 1 - h
        return h, P.n["Y%d" % h]

    def wl_mark(h, sem):
        wl_state["mark"][h].append((sem, P.n[sem]))

    def wrear(w_ap):
        return w_ap.rearrange("(c p) f -> p c f", p=128)

    slot_mark = {}

    def wait_slot(eng, key):
        if key in slot_mark:
            P.w(eng, "T", slot_mark[key])

    def set_slot(key):
        slot_mark[key] = P.n["T"]

    def load_w_dd(w_ap, dstbuf, dstkey, scale_coff=None, const_scale=None,
                  bias_rhs_coff=None, bias_dst_coff=None):
        for ci in range(DC):
            h, ydone = land(wrear(w_ap)[:, ci:ci + 1, :], 1, 1024)
            seg = WL[:, h * 1024:(h + 1) * 1024]
            od = dstbuf[:, ci * 1024:(ci + 1) * 1024]
            P.w("v", "Y%d" % h, ydone)
            if ci == 0:
                wait_slot("v", dstkey)
            if scale_coff is not None:
                vts(od, seg, COLS[:, scale_coff + ci:scale_coff + ci + 1],
                    ALU.mult)
            elif const_scale is not None:
                vts(od, seg, const_scale, ALU.mult)
            else:
                vcopy(od, seg)
            if bias_rhs_coff is not None:
                P.w("t", "Y%d" % h, ydone)
                pp = pX if ci % 2 == 0 else pE
                P.w("t", "V")
                for oc in range(8):
                    mm(pp[:, oc * 8:(oc + 1) * 8],
                       seg[:, oc * 128:(oc + 1) * 128],
                       COLS[:, bias_rhs_coff:bias_rhs_coff + 8],
                       start=True, stop=True, inc=("T" if oc == 7 else None))
                wl_mark(h, "T")
                wl_mark(h, "V")
                P.w("v", "T")
                vstt(COLS[:, bias_dst_coff:bias_dst_coff + 8],
                     COLS[:, bias_dst_coff:bias_dst_coff + 8], 1.0,
                     pp[:, 0:64].rearrange("p (o j) -> p o j", j=8)[:, :, ci],
                     ALU.mult, ALU.add)
            else:
                wl_mark(h, "V")

    def emit_dumps(names):
        allb = {"dbg_fma": FMA, "dbg_fmb": FMB, "dbg_str": STR,
                "dbg_kvb": KVB, "dbg_kvp": KVP, "dbg_cols": COLS,
                "dbg_ss": SS, "dbg_scb": SCB, "dbg_scr": SCR,
                "dbg_wa": WA, "dbg_wb": WB}
        for eng, semn in [("y", "T"), ("y", "V"), ("y", "S"), ("y", "G"),
                          ("y", "Y0"), ("y", "Y1"), ("y", "YS"),
                          ("y", "YK")]:
            P.w(eng, semn)
        for nm in names:
            if nm == "dbg_cols":
                dma("y", dbg[nm][:, 0:120], COLS[:, 0:120])
            else:
                dma("y", dbg[nm], allb[nm][:, :])
        P.w("y", "Y")

    if stop_after is not None:
        memset(SCB[:, :], 0.0)
        memset(SCR[:, :], 0.0)
        memset(SS[:, :], 0.0)
        P.w("y", "V")
    dma("y", EYEB[:, :], eyeB_in, sem="YS")
    dma("y", ONE1[0:1, :], ones_in, sem="YS")
    dma("y", SCR[0:64, 0:128], w_in["proj"], sem="YS")
    dma("y", SCR[64:128, 0:128], w_in["proj"], sem="YS")
    dma("y", COLS[:, 0:88], cols_in, sem="YS")
    dma("y", SCR[0:1, 512:2560], rows_in, sem="YS")
    for c in range(DC):
        dma("y", SCR[:, 128 + c * 8:128 + (c + 1) * 8],
            w_in["gate"][c * 128:(c + 1) * 128, :], sem="YS")
    dma("y", STR[:, :].rearrange("p (t f) -> p t f", t=NT),
        x_in.rearrange("(t p) f -> p t f", p=128), sem="YS")
    setup_y = P.n["YS"]

    memset(SCR[:, 3104:3136], 0.0)
    P.w("v", "YS", setup_y)
    memset(COLS[:, CB1P:CB1P + 32], 0.0)
    memset(stt_c(CZERO), 0.0)
    memset(stt_c(CPIO2), PI / 2)
    vcopy(PRJ[0:64, :], SCR[0:64, 0:128])
    vcopy(PRJ[64:128, :], SCR[64:128, 0:128])
    vcopy(GTB[:, :], SCR[:, 128:192])
    vcopy(ROWS[0:1, :], SCR[0:1, 512:2560])

    def barrier_all():
        for eng in ("t", "v", "s", "g", "y"):
            for semn in ("T", "V", "S", "G", "Y", "GD"):
                P.w(eng, semn)

    P.w("s", "YS", setup_y)
    xnb = SCB[:, OXN:OXN + 1024]
    for t in range(NT):
        xt = STR[:, t * D:(t + 1) * D]
        P.w("s", "T")
        ln_tile([(xt, 1024)], [xnb])
        ssettle()
        P.w("t", "S")
        P.w("t", "V")
        for c in range(DC):
            tpose(pT[:, c * 128:(c + 1) * 128], xnb[:, c * 128:(c + 1) * 128],
                  EYEB[:, :], inc=("T" if c == DC - 1 else None))
        P.w("v", "T")
        vcopy(fma_tile_view(FMA, t),
              pT[:, 0:1024].rearrange("p (c r) -> p c r", c=DC))

    if stop_after == "A":
        emit_dumps(["dbg_fma", "dbg_str"])
        return _finish(nc, stk, P, sems)

    load_w_dd(w_in["wk"], WA, "WA", scale_coff=CG1,
              bias_rhs_coff=CL1B, bias_dst_coff=CBK)
    load_w_dd(w_in["wv"], WB, "WB", scale_coff=CG1,
              bias_rhs_coff=CL1B, bias_dst_coff=CBV)
    wkv_cast = P.n["V"]
    wkv_bias = P.n["V"]
    if stop_after == "W":
        P.w("y", "G")
        emit_dumps(["dbg_cols", "dbg_wa", "dbg_wb"])
        return _finish(nc, stk, P, sems)

    KP = SCB[:, OKP:OKP + 1024]
    KPv = KP.rearrange("p (c r) -> p c r", r=256)
    vfm = SCB[:, OVFM:OVFM + 512]
    VPb = SCB[:, OVP:OVP + 512]
    kAb = SCB[:, OQA:OQA + 512]
    kBb = SCB[:, OQB:OQB + 512]

    def fourier(src_psum, sink_cos, sink_sin):
        u = SCR[:, OU:OU + 512]
        v = SCR[:, OV:OV + 512]
        r = SCR[:, OR_:OR_ + 512]
        a = SCR[:, OA:OA + 512]
        P.w("v", "S")
        vts(u, src_psum, 1.0 / (16 * PI), ALU.mult, MAGIC, ALU.add)
        vts(v, u, MAGIC, ALU.subtract, -2 * PI, ALU.mult)
        vstt(r, src_psum, 0.125, v, ALU.mult, ALU.add)
        vsettle()
        P.w("s", "V")
        P.w("s", "T")
        act(a, r, AF.Abs, zero)
        act(sink_cos, a, AF.Sin, stt_c(CPIO2), scale=-1.0)
        act(sink_sin, r, AF.Sin, zero)
        ssettle()

    P.w("t", "V", max(wkv_cast, wkv_bias))
    bank_mark = {}
    it_b = 0
    for pr in range(HP):
        memset(KVA[:, :], 0.0)
        for tc in range(4):
            kbank = pA if it_b % 2 == 0 else pF
            vbank = pC if it_b % 2 == 0 else pE
            P.w("t", "S", bank_mark.get(("k", it_b % 2), 0))
            for c in range(DC):
                mm(kbank[:, :], WA[:, c * 1024 + pr * 128:c * 1024 + (pr + 1) * 128],
                   FMA[:, c * 2048 + tc * 512:c * 2048 + (tc + 1) * 512],
                   start=(c == 0), stop=(c == DC - 1),
                   inc=("T" if c == DC - 1 else None))
            P.w("t", "S", bank_mark.get(("v", it_b % 2), 0))
            for c in range(DC):
                mm(vbank[:, :], WB[:, c * 1024 + pr * 128:c * 1024 + (pr + 1) * 128],
                   FMA[:, c * 2048 + tc * 512:c * 2048 + (tc + 1) * 512],
                   start=(c == 0), stop=(c == DC - 1),
                   inc=("T" if c == DC - 1 else None))
            k_mm = P.n["T"] - 1
            P.w("s", "T", k_mm)
            act(kAb[0:64, :], kbank[0:64, :], AF.Identity,
                COLS[0:64, CBK + pr:CBK + pr + 1])
            act(kBb[64:128, :], kbank[64:128, :], AF.Identity,
                COLS[64:128, CBK + pr:CBK + pr + 1])
            ssettle()
            k_evac = P.n["S"]
            bank_mark[("k", it_b % 2)] = k_evac
            P.w("s", "T")
            act(vfm[:, :], vbank[:, :], AF.Identity,
                COLS[:, CBV + pr:CBV + pr + 1])
            ssettle()
            v_evac = P.n["S"]
            bank_mark[("v", it_b % 2)] = v_evac
            it_b += 1
            P.w("t", "S", v_evac)
            P.w("t", "V")
            for ttc in range(4):
                tpose(pT[:, ttc * 128:(ttc + 1) * 128],
                      vfm[:, ttc * 128:(ttc + 1) * 128], EYEB[:, :],
                      inc=("T" if ttc == 3 else None))
            P.w("v", "T")
            vcopy(VPb[:, :], pT[:, 0:512])
            vsettle()
            vp_ready = P.n["V"]
            for hh in range(2):
                P.w("t", "S", k_evac)
                P.w("t", "V")
                for ttc in range(4):
                    if hh == 0:
                        mm(pB[:, ttc * 128:(ttc + 1) * 128],
                           kAb[0:64, ttc * 128:(ttc + 1) * 128],
                           PRJ[0:64, :], start=(ttc == 0), stop=(ttc == 3),
                           inc=("T" if ttc == 3 else None))
                    else:
                        mm(pB[:, ttc * 128:(ttc + 1) * 128],
                           kBb[64:128, ttc * 128:(ttc + 1) * 128],
                           PRJ[64:128, :], start=(ttc == 0), stop=(ttc == 3),
                           inc=("T" if ttc == 3 else None))
                P.w("v", "T")
                fourier(pB[:, :], KPv[:, :, 0:128], KPv[:, :, 128:256])
                kp_ready = P.n["S"]
                P.w("t", "S", kp_ready)
                P.w("t", "V", vp_ready)
                for ttc in range(4):
                    lh = VPb[:, ttc * 128 + hh * 64:ttc * 128 + hh * 64 + 64]
                    if hh == 0:
                        mm(pD[0:64, 0:256], lh, KP[:, ttc * 256:(ttc + 1) * 256],
                           start=(ttc == 0), stop=(ttc == 3),
                           inc=("T" if ttc == 3 else None))
                    else:
                        mm(pD[64:128, 0:256], lh, KP[:, ttc * 256:(ttc + 1) * 256],
                           start=(ttc == 0), stop=(ttc == 3),
                           tile_position=(0, 64),
                           inc=("T" if ttc == 3 else None))
            P.w("v", "T")
            vtt(KVA[:, :], pD[:, 0:256], KVA[:, :], ALU.add)
            P.w("t", "V")
        vsettle()
        vts(KVP[:, pr * 256:(pr + 1) * 256], KVA[:, :], 1.0, ALU.mult)
        if stop_after == "B0" and pr == 0:
            emit_dumps(["dbg_kvp", "dbg_scb", "dbg_scr"])
            return _finish(nc, stk, P, sems)
    kvp_done = P.n["V"]
    set_slot("WA")
    set_slot("WB")
    if stop_after == "B1":
        emit_dumps(["dbg_kvp", "dbg_fma"])
        return _finish(nc, stk, P, sems)

    P.w("g", "V", kvp_done)
    dma("g", cc_in_h.ap(), KVP[:, :])
    P.w("g", "GD")
    P.r("g", lambda e: e.collective_compute(
        "AllReduce", ALU.add,
        replica_groups=[[0, 1, 2, 3], [4, 5, 6, 7]],
        ins=[cc_in_h.ap().opt()], outs=[cc_out_h.ap().opt()]), inc="CC")

    load_w_dd(w_in["wq"], WA, "WA", scale_coff=CG1,
              bias_rhs_coff=CL1B, bias_dst_coff=CBQ)
    load_w_dd(w_in["wo"], WB, "WB", const_scale=1.0 / 128.0)
    wqo_cast = P.n["V"]
    wq_bias = P.n["V"]

    P.w("y", "CC")
    dma("y", KVP[0:64, :], cc_out_h.ap()[0:64, :], sem="YK")
    dma("y", SCB[0:64, 0:2048], cc_out_h.ap()[64:128, :], sem="YK")
    kvsum_y = P.n["YK"]

    P.w("t", "YK", kvsum_y)
    for half in range(4):
        P.w("t", "V")
        for i in range(8):
            gidx = half * 8 + i
            h = gidx // 2
            mc = gidx % 2
            pr = h // 2
            hh = h % 2
            buf = KVP if hh == 0 else SCB
            src = buf[0:64,
                      pr * 256 + mc * 128:pr * 256 + (mc + 1) * 128]
            idn = EYEB[0:64, 0:64]
            tpose(pT[:, i * 64:(i + 1) * 64], src, idn,
                  inc=("T" if i == 7 else None))
        P.w("v", "T")
        vcopy(KVB[:, half * 512:(half + 1) * 512], pT[:, 0:512])
    kvb_done = P.n["V"]

    if stop_after == "B":
        emit_dumps(["dbg_fma", "dbg_str", "dbg_kvb", "dbg_kvp", "dbg_cols"])
        return _finish(nc, stk, P, sems)

    ATT = SCB[:, OATT:OATT + 4096]
    QPC = SCB[:, OQPC:OQPC + 512]
    QPS = SCB[:, OQPS:OQPS + 512]
    qAb = SCB[:, OQA:OQA + 512]
    qBb = SCB[:, OQB:OQB + 512]
    xn2 = SCB[:, OXN:OXN + 1024]

    P.w("t", "V", max(wqo_cast, wq_bias, kvb_done))
    qbank_mark = {}
    it_c = 0
    for tc in range(4):
        P.w("v", "S")
        for pr in range(HP):
            qbank = pA if it_c % 2 == 0 else pF
            P.w("t", "S", qbank_mark.get(it_c % 2, 0))
            for c in range(DC):
                mm(qbank[:, :], WA[:, c * 1024 + pr * 128:c * 1024 + (pr + 1) * 128],
                   FMA[:, c * 2048 + tc * 512:c * 2048 + (tc + 1) * 512],
                   start=(c == 0), stop=(c == DC - 1),
                   inc=("T" if c == DC - 1 else None))
            P.w("s", "T")
            act(qAb[0:64, :], qbank[0:64, :], AF.Identity,
                COLS[0:64, CBQ + pr:CBQ + pr + 1])
            act(qBb[64:128, :], qbank[64:128, :], AF.Identity,
                COLS[64:128, CBQ + pr:CBQ + pr + 1])
            ssettle()
            q_evac = P.n["S"]
            qbank_mark[it_c % 2] = q_evac
            it_c += 1
            for hh in range(2):
                hgl = 2 * pr + hh
                P.w("t", "S", q_evac)
                P.w("t", "V")
                if hh == 0:
                    mm(pB[:, :], PRJ[0:64, :], qAb[0:64, :],
                       start=True, stop=True, inc="T")
                else:
                    mm(pB[:, :], PRJ[64:128, :], qBb[64:128, :],
                       start=True, stop=True, inc="T")
                P.w("v", "T")
                if stop_after == "C1" and tc == 0 and pr == 0 and hh == 0:
                    vcopy(SCR[:, 2048:2560], pB[:, :])
                    vcopy(SCR[:, 2560:2688], PRJ[:, :])
                    vcopy(SCR[:, 2688:2816], pA[:, 0:128])
                    emit_dumps(["dbg_scb", "dbg_str", "dbg_fma", "dbg_scr", "dbg_wa", "dbg_wb"])
                    return _finish(nc, stk, P, sems)
                fourier(pB[:, :], QPC, QPS)
                qp_ready = P.n["S"]
                P.w("t", "S", qp_ready)
                if hh == 0:
                    mm(pC[0:64, :], KVB[:, hgl * 128:hgl * 128 + 64], QPC,
                       start=True, stop=False)
                    mm(pC[0:64, :], KVB[:, hgl * 128 + 64:hgl * 128 + 128], QPS,
                       start=False, stop=True, inc="T")
                else:
                    mm(pC[64:128, :], KVB[:, hgl * 128:hgl * 128 + 64], QPC,
                       start=True, stop=False, tile_position=(0, 64))
                    mm(pC[64:128, :], KVB[:, hgl * 128 + 64:hgl * 128 + 128], QPS,
                       start=False, stop=True, tile_position=(0, 64), inc="T")
            P.w("s", "T")
            act(ATT[:, pr * 512:(pr + 1) * 512], pC[:, :], AF.Identity, zero)
            ssettle()
        att_ready = P.n["S"]
        for tsub in range(4):
            t = tc * 4 + tsub
            P.w("t", "S", att_ready)
            P.w("t", "V")
            for nh in range(2):
                pp = pD if nh == 0 else pE
                mm(pp[:, :], ONE1[0:1, :], ROWS[0:1, nh * 512:(nh + 1) * 512],
                   start=True, stop=False)
                for pr in range(HP):
                    mm(pp[:, :],
                       ATT[:, pr * 512 + tsub * 128:pr * 512 + (tsub + 1) * 128],
                       WB[:, pr * 1024 + nh * 512:pr * 1024 + (nh + 1) * 512],
                       start=False, stop=(pr == HP - 1),
                       inc=("T" if pr == HP - 1 else None))
            P.w("v", "T")
            Asc = SCR[:, OASC:OASC + 1024]
            vtt(Asc[:, 0:512], pD[:, :], STR[:, t * D:t * D + 512], ALU.add)
            vtt(Asc[:, 512:1024], pE[:, :], STR[:, t * D + 512:(t + 1) * D],
                ALU.add)
            vtt(STR[:, t * D:(t + 1) * D], Asc[:, :],
                STR[:, t * D:(t + 1) * D], ALU.add)
            P.w("s", "V")
            P.w("s", "T")
            ln_tile([(Asc, 1024)], [xn2])
            ssettle()
            P.w("t", "S")
            P.w("t", "V")
            for c in range(DC):
                tpose(pT[:, c * 128:(c + 1) * 128],
                      xn2[:, c * 128:(c + 1) * 128], EYEB[:, :],
                      inc=("T" if c == DC - 1 else None))
            P.w("v", "T")
            vcopy(fma_tile_view(FMB, t),
                  pT[:, 0:1024].rearrange("p (c r) -> p c r", c=DC))

    set_slot("WA")
    set_slot("WB")

    if stop_after == "C":
        emit_dumps(["dbg_fma", "dbg_fmb", "dbg_str", "dbg_kvb", "dbg_cols",
                    "dbg_scb", "dbg_wa", "dbg_wb"])
        return _finish(nc, stk, P, sems)

    P.w("v", "T")
    vcopy(COLS[:, CB1P:CB1P + 32], COLS[:, CB1:CB1 + 32])
    HB = FMA
    for tq in range(4):
        for hs in range(DC):
            slot = hs % 2
            wslot = WA[:, slot * 4096:(slot + 1) * 4096]
            for lnd in range(4):
                h, ydone = land(
                    wrear(w_in["w1"])[:, lnd * 2:lnd * 2 + 2,
                                      hs * 512:(hs + 1) * 512], 2, 512)
                P.w("v", "Y%d" % h, ydone)
                if lnd == 0:
                    wait_slot("v", ("W1", slot))
                    wait_slot("v", "WA")
                for j in range(2):
                    ci = lnd * 2 + j
                    seg = WL[:, h * 1024 + j * 512:h * 1024 + (j + 1) * 512]
                    vts(wslot[:, ci * 512:(ci + 1) * 512], seg,
                        COLS[:, CG2 + ci:CG2 + ci + 1], ALU.mult)
                if tq == 0:
                    P.w("t", "Y%d" % h, ydone)
                    for j in range(2):
                        ci = lnd * 2 + j
                        seg = WL[:, h * 1024 + j * 512:h * 1024 + (j + 1) * 512]
                        pp = pX if ci % 2 == 0 else pC
                        P.w("t", "V")
                        for hc in range(4):
                            mm(pp[:, hc * 8:(hc + 1) * 8],
                               seg[:, hc * 128:(hc + 1) * 128],
                               COLS[:, CL2B:CL2B + 8],
                               start=True, stop=True,
                               inc=("T" if hc == 3 else None))
                        P.w("v", "T")
                        vstt(COLS[:, CB1P + hs * 4:CB1P + hs * 4 + 4],
                             COLS[:, CB1P + hs * 4:CB1P + hs * 4 + 4], 1.0,
                             pp[:, 0:32].rearrange(
                                 "p (o j) -> p o j", j=8)[:, :, ci],
                             ALU.mult, ALU.add)
                    wl_mark(h, "T")
                    wl_mark(h, "V")
                else:
                    wl_mark(h, "V")
            w1_ready = P.n["V"]
            P.w("t", "V", w1_ready)
            for hc in range(4):
                hcg = hs * 4 + hc
                pp = pA if hcg % 2 == 0 else pB
                P.w("t", "S", bank_mark.get(("dh", hcg % 2), 0))
                for c in range(DC):
                    mm(pp[:, :],
                       wslot[:, c * 512 + hc * 128:c * 512 + (hc + 1) * 128],
                       FMB[:, c * 2048 + tq * 512:c * 2048 + (tq + 1) * 512],
                       start=(c == 0), stop=(c == DC - 1),
                       inc=("T" if c == DC - 1 else None))
                P.w("s", "T")
                act(HB[:, hcg * 512:(hcg + 1) * 512], pp[:, :], AF.Relu,
                    COLS[:, CB1P + hcg:CB1P + hcg + 1])
                bank_mark[("dh", hcg % 2)] = P.n["S"]
            set_slot(("W1", slot))
        ssettle()
        h_done_s = P.n["S"]
        for nh in range(2):
            P.w("t", "V")
            for hs in range(DC):
                slot = hs % 2
                w2slot = WB[:, slot * 2048:(slot + 1) * 2048]
                for lnd in range(2):
                    h, ydone = land(
                        wrear(w_in["w2"])[:, hs * 4 + lnd * 2:hs * 4 + lnd * 2 + 2,
                                          nh * 512:(nh + 1) * 512], 2, 512)
                    P.w("v", "Y%d" % h, ydone)
                    if lnd == 0:
                        wait_slot("v", ("W2", slot))
                        wait_slot("v", "WB")
                    for j in range(2):
                        hsub = lnd * 2 + j
                        seg = WL[:, h * 1024 + j * 512:h * 1024 + (j + 1) * 512]
                        vcopy(w2slot[:, hsub * 512:(hsub + 1) * 512], seg)
                    wl_mark(h, "V")
                w2_ready = P.n["V"]
                P.w("t", "V", w2_ready)
                if hs == 0:
                    P.w("t", "S", h_done_s)
                for tsub in range(4):
                    pp = [pC, pD, pE, pF][tsub]
                    if hs == 0:
                        mm(pp[:, :], ONE1[0:1, :],
                           ROWS[0:1, 1024 + nh * 512:1024 + (nh + 1) * 512],
                           start=True, stop=False)
                    for hsub in range(4):
                        hcg = hs * 4 + hsub
                        mm(pp[:, :],
                           HB[:, hcg * 512 + tsub * 128:hcg * 512 + (tsub + 1) * 128],
                           w2slot[:, hsub * 512:(hsub + 1) * 512],
                           start=False, stop=(hs == DC - 1 and hsub == 3),
                           inc=("T" if hsub == 3 else None))
                set_slot(("W2", slot))
            for tsub in range(4):
                t = tq * 4 + tsub
                pp = [pC, pD, pE, pF][tsub]
                P.w("v", "T")
                if nh == 0:
                    vtt(SCR[:, tsub * 512:(tsub + 1) * 512], pp[:, :],
                        STR[:, t * D:t * D + 512], ALU.add)
                else:
                    vtt(SCR[:, ODMP:ODMP + 512], pp[:, :],
                        STR[:, t * D + 512:(t + 1) * D], ALU.add)
                    P.w("s", "V")
                    ln_tile([(SCR[:, tsub * 512:(tsub + 1) * 512], 512),
                             (SCR[:, ODMP:ODMP + 512], 512)],
                            [STR[:, t * D:t * D + 512],
                             STR[:, t * D + 512:(t + 1) * D]],
                            dump_off=ODMP + 512)
                    P.w("v", "S")

    set_slot("FMB")

    if stop_after == "D":
        emit_dumps(["dbg_str", "dbg_cols"])
        return _finish(nc, stk, P, sems)

    ybs = SCB[:, OXN:OXN + 1024]
    for t in range(NT):
        P.w("v", "S")
        P.w("v", "T")
        vts(ybs[:, :], STR[:, t * D:(t + 1) * D], 1.0, ALU.mult)
        P.w("t", "V")
        for c in range(DC):
            tpose(pT[:, c * 128:(c + 1) * 128], ybs[:, c * 128:(c + 1) * 128],
                  EYEB[:, :], inc=("T" if c == DC - 1 else None))
        P.w("v", "T")
        vcopy(fma_tile_view(FMA, t),
              pT[:, 0:1024].rearrange("p (c r) -> p c r", c=DC))
    vsettle()
    y1t_done = P.n["V"]

    P.w("t", "V", y1t_done)
    for tt in range(NT):
        P.w("t", "S")
        for c in range(DC):
            mm(pX[:, 0:8], FMA[:, c * 2048 + tt * 128:c * 2048 + (tt + 1) * 128],
               GTB[:, c * 8:(c + 1) * 8], start=(c == 0), stop=(c == DC - 1),
               inc=("T" if c == DC - 1 else None))
        P.w("v", "T")
        ssl = SS[:, tt * 8:(tt + 1) * 8]
        vred(stt_c(CG_A), pX[:, 0:8], ALU.max)
        vsettle()
        vts(stt_c(CG_B), stt_c(CG_A), -1.0, ALU.mult)
        vsettle()
        P.w("s", "V")
        act(SCR[:, OG1:OG1 + 8], pX[:, 0:8], AF.Exp, stt_c(CG_B))
        ssettle()
        P.w("v", "S")
        vred(stt_c(CG_A), SCR[:, OG1:OG1 + 8], ALU.add)
        vsettle()
        vrecip(stt_c(CG_B), stt_c(CG_A))
        vsettle()
        vts(ssl, SCR[:, OG1:OG1 + 8], stt_c(CG_B), ALU.mult)
        vsettle()
        vred(stt_c(CG_C), ssl, ALU.max)
        vsettle()
        vts(SCR[:, OG2:OG2 + 8], ssl, stt_c(CG_C), ALU.is_ge)
        vsettle()
        vstt(SCR[:, OG3:OG3 + 8], SCR[:, OG2:OG2 + 8], -1e30, ssl,
             ALU.mult, ALU.add)
        vsettle()
        vred(stt_c(CG_D), SCR[:, OG3:OG3 + 8], ALU.max)
        vsettle()
        vts(SCR[:, OG2:OG2 + 8], ssl, stt_c(CG_D), ALU.is_ge)
        vsettle()
        vtt(ssl, ssl, SCR[:, OG2:OG2 + 8], ALU.mult)
    gate_done = P.n["V"]

    h1b = [SCB[:, OH1:OH1 + 512], SCB[:, 2048 + OH1:2048 + OH1 + 512]]
    a3b = [SCB[:, OA3:OA3 + 512], SCB[:, 2048 + OA3:2048 + OA3 + 512]]
    hsb = [SCB[:, OHS:OHS + 512], SCB[:, 2048 + OHS:2048 + OHS + 512]]
    hfb = [SCB[:, OHF:OHF + 512], SCB[:, 2048 + OHF:2048 + OHF + 512]]
    P.w("s", "V", gate_done)
    a3_mark = {0: 0, 1: 0}
    hf_mark = {0: 0, 1: 0}
    tp_mark = {0: 0, 1: 0}
    last_silu = [0]
    last_a3 = [0]
    pending = None

    def emit_tail(nxt_raws_done):
        par, grp, tq, ei, tsub, hsb_v, _ = pending
        gp = grp % 2
        e = grp * 2 + ei
        P.w("t", "V", hsb_v)
        for i4 in range(4):
            tpose(pT[:, par * 512 + i4 * 128:par * 512 + (i4 + 1) * 128],
                  hsb[par][:, i4 * 128:(i4 + 1) * 128], EYEB[:, :],
                  inc=("T" if i4 == 3 else None))
        tp_mark[par] = P.n["T"]
        P.w("v", "T", tp_mark[par])
        vcopy(hfb[par], pT[:, par * 512:par * 512 + 512])
        vsettle()
        hf_mark[par] = P.n["V"]
        P.w("t", "V", hf_mark[par])
        for nh in range(2):
            pp = [pC, pD, pE, pF][tsub * 2 + nh]
            for hc in range(4):
                w2b = gp * 8192 + ei * 4096 + hc * 1024 + nh * 512
                mm(pp[:, :], hfb[par][:, hc * 128:(hc + 1) * 128],
                   FMB[:, w2b:w2b + 512],
                   start=(ei == 0 and hc == 0),
                   stop=(ei == 1 and hc == 3),
                   inc=("T" if hc == 3 else None))
        return P.n["T"]

    evac_pending = []

    def emit_evac(grp_tq):
        tq = grp_tq
        for tsub in range(2):
            tt = tq * 2 + tsub
            for nh in range(2):
                pp = [pC, pD, pE, pF][tsub * 2 + nh]
                P.w("v", "T")
                vtt(STR[:, tt * D + nh * 512:tt * D + (nh + 1) * 512],
                    pp[:, :],
                    STR[:, tt * D + nh * 512:tt * D + (nh + 1) * 512],
                    ALU.add)

    itidx = 0
    for grp in range(4):
        gp = grp % 2
        for ei in range(2):
            e = grp * 2 + ei
            for src_nm, dbuf, base, key in [
                    ("mw1", WA, ei * 4096, ("EW1", ei)),
                    ("mw3", WB, ei * 4096, ("EW3", ei))]:
                wsl = w_in[src_nm][e * D:(e + 1) * D, :]
                for lnd in range(4):
                    h, ydone = land(wrear(wsl)[:, lnd * 2:lnd * 2 + 2, :], 2, 512)
                    P.w("v", "Y%d" % h, ydone)
                    if lnd == 0:
                        wait_slot("v", key)
                        if src_nm == "mw1":
                            wait_slot("v", ("W1", ei))
                        else:
                            wait_slot("v", ("W2", 0))
                            wait_slot("v", ("W2", 1))
                            wait_slot("v", "WB")
                    for j in range(2):
                        ci = lnd * 2 + j
                        seg = WL[:, h * 1024 + j * 512:h * 1024 + (j + 1) * 512]
                        vcopy(dbuf[:, base + ci * 512:base + (ci + 1) * 512],
                              seg)
                    wl_mark(h, "V")
            w2sl = w_in["mw2"][e * HE:(e + 1) * HE, :]
            fbase = gp * 8192 + ei * 4096
            for lnd in range(4):
                h, ydone = land(wrear(w2sl)[:, lnd:lnd + 1, :], 1, 1024)
                P.w("v", "Y%d" % h, ydone)
                if lnd == 0:
                    wait_slot("v", ("EW2", gp, ei))
                    wait_slot("v", "FMB")
                seg = WL[:, h * 1024:(h + 1) * 1024]
                vcopy(FMB[:, fbase + lnd * 1024:fbase + (lnd + 1) * 1024],
                      seg)
                wl_mark(h, "V")
        ew_cast = P.n["V"]
        P.w("t", "V", ew_cast)
        for tq in range(8):
            for ei in range(2):
                e = grp * 2 + ei
                for tsub in range(2):
                    par = itidx % 2
                    tt = tq * 2 + tsub
                    toff = tq * 256 + tsub * 128
                    P.w("t", "S", last_silu[0])
                    P.w("t", "V", last_a3[0])
                    for c in range(DC):
                        mm(pA[:, :], FMA[:, c * 2048 + toff:c * 2048 + toff + 128],
                           WA[:, ei * 4096 + c * 512:ei * 4096 + (c + 1) * 512],
                           start=(c == 0), stop=(c == DC - 1),
                           inc=("T" if c == DC - 1 else None))
                    for c in range(DC):
                        mm(pB[:, :], FMA[:, c * 2048 + toff:c * 2048 + toff + 128],
                           WB[:, ei * 4096 + c * 512:ei * 4096 + (c + 1) * 512],
                           start=(c == 0), stop=(c == DC - 1),
                           inc=("T" if c == DC - 1 else None))
                    raws_done = P.n["T"]
                    P.w("s", "T", raws_done)
                    P.w("s", "V", hf_mark[par])
                    sse = SS[:, tt * 8 + e:tt * 8 + e + 1]
                    act(h1b[par], pA[:, :], AF.Silu, zero, scale=sse)
                    silu_s = P.n["S"]
                    last_silu[0] = silu_s
                    P.w("v", "S", silu_s)
                    P.w("v", "T", tp_mark[par])
                    vts(a3b[par], pB[:, :], sse, ALU.mult)
                    last_a3[0] = P.n["V"]
                    vtt(hsb[par], h1b[par], a3b[par], ALU.mult)
                    vsettle()
                    a3_mark[par] = P.n["V"]
                    hsb_v = P.n["V"]
                    prev = pending
                    if prev is not None:
                        emit_tail(raws_done)
                        if prev[3] == 1 and prev[4] == 1:
                            emit_evac(prev[2])
                    pending = (par, grp, tq, ei, tsub, hsb_v, silu_s)
                    itidx += 1
        for ei2 in range(2):
            set_slot(("EW1", ei2))
            set_slot(("EW3", ei2))
            set_slot(("EW2", gp, ei2))
    if pending is not None:
        emit_tail(P.n["T"])
        emit_evac(pending[2])
        pending = None
    moe_done = P.n["V"]

    out_mark = [0, 0]
    P.w("s", "V", moe_done)
    for t in range(NT):
        png = t % 2
        if out_mark[png] > 0:
            P.w("s", "YO%d" % png, out_mark[png])
        ln_tile([(STR[:, t * D:(t + 1) * D], 1024)],
                [SCR[:, png * 1024:(png + 1) * 1024]])
        ssettle()
        s_done = P.n["S"]
        P.w("y", "S", s_done)
        dma("y", y_out[t * 128:(t + 1) * 128, :],
            SCR[:, png * 1024:(png + 1) * 1024], sem="YO%d" % png)
        out_mark[png] = P.n["YO%d" % png]

    return _finish(nc, stk, P, sems)


def _finish(nc, stk, P, sems):
    for semn in ("Y", "Y0", "Y1", "YS", "YK", "YO0", "YO1"):
        P.w("y", semn)
    block = stk.enter_context(nc.Block())

    def replay(engname):
        def run(e):
            for step in P.q[engname]:
                if step[0] == "w":
                    e.wait_ge(sems[step[1]], step[2])
                else:
                    instr = step[1](e)
                    if step[2] is not None:
                        instr.then_inc(sems[step[2]], step[3])
        return run

    block.tensor(replay("t"))
    block.vector(replay("v"))
    block.scalar(replay("s"))
    block.gpsimd(replay("g"))
    block.sync(replay("y"))
    stk.close()
    return nc


def make_inputs(inputs):
    f32 = lambda a: np.ascontiguousarray(np.asarray(a), dtype=np.float32)
    cols = np.concatenate([
        f32(inputs["L0_attn_ln1_g"]).reshape(8, 128).T,
        f32(inputs["L0_attn_ln1_b"]).reshape(8, 128).T,
        f32(inputs["L0_attn_ln2_g"]).reshape(8, 128).T,
        f32(inputs["L0_attn_ln2_b"]).reshape(8, 128).T,
        f32(inputs["L0_attn_b1"]).reshape(32, 128).T,
        f32(inputs["L0_attn_bq"]).reshape(8, 128).T,
        f32(inputs["L0_attn_bk"]).reshape(8, 128).T,
        f32(inputs["L0_attn_bv"]).reshape(8, 128).T,
    ], axis=1)
    rows = np.concatenate([
        f32(inputs["L0_attn_bo"]).reshape(1, 1024),
        f32(inputs["L0_attn_b2"]).reshape(1, 1024),
    ], axis=1)
    base = {
        "wq": f32(inputs["L0_attn_wq"]),
        "wk": f32(inputs["L0_attn_wk"]),
        "wv": f32(inputs["L0_attn_wv"]),
        "wo": f32(inputs["L0_attn_wo"]),
        "w1": f32(inputs["L0_attn_w1"]),
        "w2": f32(inputs["L0_attn_w2"]),
        "proj": f32(inputs["L0_attn_proj_mat"]),
        "gate": f32(inputs["L0_ffn_gate_w"]),
        "mw1": f32(inputs["L0_ffn_w1"]).reshape(NE * D, HE),
        "mw3": f32(inputs["L0_ffn_w3"]).reshape(NE * D, HE),
        "mw2": f32(inputs["L0_ffn_w2"]).reshape(NE * HE, D),
        "colsin": np.ascontiguousarray(cols),
        "rowsin": np.ascontiguousarray(rows),
        "eyeB": np.eye(128, dtype=BFNP),
        "ones1": np.ones((1, 128), dtype=BFNP),
    }
    x = np.asarray(inputs["L0_attn_x"], dtype=np.float32).reshape(16384, D)
    shards = [np.ascontiguousarray(x[i * T:(i + 1) * T]) for i in range(8)]
    return base, shards


_NC_CACHE = {}


def kernel(**inputs):
    if "nc" not in _NC_CACHE:
        _NC_CACHE["nc"] = build_nc()
    nc = _NC_CACHE["nc"]
    base, shards = make_inputs(inputs)
    in_maps = [dict(base, x=shards[i]) for i in range(8)]
    res = run_bass_kernel_spmd(nc, in_maps, core_ids=list(range(8)))
    out = np.concatenate([np.asarray(res.results[c]["y"]) for c in range(8)],
                         axis=0)
    return out.reshape(2, 8192, 1024)


# version 09 (final)
# speedup vs baseline: 1.5724x; 1.0486x over previous
import math
import contextlib

import numpy as np
import ml_dtypes

import concourse.bass as bass
import concourse.mybir as mybir
from concourse.bass_utils import run_bass_kernel_spmd

F32 = mybir.dt.float32
BF16 = mybir.dt.bfloat16
AF = mybir.ActivationFunctionType
ALU = mybir.AluOpType
AX = mybir.AxisListType

MAGIC = 12582912.0
PI = math.pi
EPS = 1e-5

T = 2048
NT = 16
D = 1024
DC = 8
HP = 8
DI = 4096
NE = 8
HE = 512
BFNP = ml_dtypes.bfloat16


class Plan:
    ENGS = ("t", "v", "s", "g", "y")

    def __init__(self):
        self.q = {e: [] for e in self.ENGS}
        self.n = {nm: 0 for nm in ("T", "V", "S", "G", "Y", "GD", "CC",
                                   "Y0", "Y1", "YS", "YK", "YO0", "YO1")}

    def w(self, eng, sem, val=None):
        v = self.n[sem] if val is None else val
        if v > 0:
            self.q[eng].append(("w", sem, v))

    def r(self, eng, fn, inc=None, k=1):
        if inc is not None:
            self.n[inc] += k
        self.q[eng].append(("r", fn, inc, k))
        return self.n[inc] if inc is not None else None


def build_nc(stop_after=None):
    nc = bass.Bass("TRN2", target_bir_lowering=False)
    nc.detect_race_conditions = False

    dt = nc.dram_tensor
    x_in = dt("x", [T, D], F32, kind="ExternalInput").ap()
    w_in = {}
    for nm, sh in [("wq", [D, D]), ("wk", [D, D]), ("wv", [D, D]),
                   ("wo", [D, D]), ("w1", [D, DI]), ("w2", [DI, D]),
                   ("proj", [64, 128]), ("gate", [D, NE]),
                   ("mw1", [NE * D, HE]), ("mw3", [NE * D, HE]),
                   ("mw2", [NE * HE, D])]:
        w_in[nm] = dt(nm, sh, F32, kind="ExternalInput").ap()
    cols_in = dt("colsin", [128, 88], F32, kind="ExternalInput").ap()
    rows_in = dt("rowsin", [1, 2048], F32, kind="ExternalInput").ap()
    eyeB_in = dt("eyeB", [128, 128], BF16, kind="ExternalInput").ap()
    ones_in = dt("ones1", [1, 128], BF16, kind="ExternalInput").ap()

    y_out = dt("y", [T, D], F32, kind="ExternalOutput").ap()
    dbg = {}
    if stop_after is not None:
        for nm, sh, ddt in [("dbg_fma", [128, 16384], BF16),
                            ("dbg_fmb", [128, 16384], BF16),
                            ("dbg_str", [128, 16384], F32),
                            ("dbg_kvb", [128, 2048], BF16),
                            ("dbg_kvp", [128, 2048], BF16),
                            ("dbg_cols", [128, 128], F32),
                            ("dbg_ss", [128, 128], F32),
                            ("dbg_scb", [128, 6144], BF16),
                            ("dbg_scr", [128, 3136], F32),
                            ("dbg_wa", [128, 8192], BF16),
                            ("dbg_wb", [128, 8192], BF16)]:
            dbg[nm] = dt(nm, sh, ddt, kind="ExternalOutput").ap()

    cc_in_h = dt("cc_in", [128, 2048], BF16)
    cc_out_h = dt("cc_out", [128, 2048], BF16)

    stk = contextlib.ExitStack()
    sb = lambda n, s, d: stk.enter_context(nc.sbuf_tensor(n, s, d))
    psal = lambda n, s, d: stk.enter_context(nc.psum_tensor(n, s, d))

    STR = sb("STR", [128, NT * D], F32)
    FMA = sb("FMA", [128, 16384], BF16)
    FMB = sb("FMB", [128, 16384], BF16)
    WA = sb("WA", [128, 8192], BF16)
    WB = sb("WB", [128, 8192], BF16)
    WL = sb("WL", [128, 2048], F32)
    SCR = sb("SCR", [128, 3136], F32)
    SCB = sb("SCB", [128, 6144], BF16)
    KVB = sb("KVB", [128, 2048], BF16)
    KVP = sb("KVP", [128, 2048], BF16)
    KVA = sb("KVA", [128, 256], F32)
    COLS = sb("COLS", [128, 128], F32)
    ROWS = sb("ROWSb", [1, 2048], BF16)
    EYEB = sb("EYEB", [128, 128], BF16)
    ONE1 = sb("ONE1", [1, 128], BF16)
    PRJ = sb("PRJ", [128, 128], BF16)
    GTB = sb("GTB", [128, 64], BF16)
    SS = sb("SSb", [128, 128], F32)
    STT = sb("STTc", [128, 16], F32)

    OU, OV, OR_, OA = 0, 512, 1024, 1536
    OASC = 0
    ODMP = 2048
    OG1, OG2, OG3 = 3072, 3080, 3088
    OATT = 0
    OKP, OVFM, OVP = 0, 1024, 1536
    OH1, OA3, OHS, OHF = 0, 512, 1024, 1536
    OQPC, OQPS = 4096, 4608
    OQA, OQB = 5120, 5632
    OXN = 5120
    CG1, CL1B, CG2, CL2B = 0, 8, 16, 24
    CB1 = 32
    CBQ, CBK, CBV = 64, 72, 80
    CB1P = 88
    (CSU0, CSU1, CSQ0, CSQ1, CMEAN, CMSQE, CVARP, CSD, CRSTD, CNMUR,
     CZERO, CPIO2, CG_A, CG_B, CG_C, CG_D) = range(16)

    pA = psal("pA", [128, 512], F32)
    pB = psal("pB", [128, 512], F32)
    pC = psal("pC", [128, 512], F32)
    pD = psal("pD", [128, 512], F32)
    pE = psal("pE", [128, 512], F32)
    pF = psal("pF", [128, 512], F32)
    pT = psal("pT", [128, 1024], BF16)
    pX = psal("pX", [128, 512], F32)

    sems = {nm: stk.enter_context(nc.semaphore("sem" + nm))
            for nm in ("T", "V", "S", "G", "Y", "GD", "CC",
                       "Y0", "Y1", "YS", "YK", "YO0", "YO1")}

    P = Plan()
    zero = STT[:, CZERO:CZERO + 1]

    def mm(out, lhsT, rhs, start, stop, tile_position=None, inc=None):
        def f(e, o=out, l=lhsT, r=rhs, st=start, sp=stop, tp=tile_position):
            return e.matmul(out=o, lhsT=l, rhs=r, start=st, stop=sp,
                            skip_group_check=True, tile_position=tp)
        P.r("t", f, inc=inc)

    def tpose(out, in_, ident, inc=None):
        P.r("t", lambda e, o=out, i=in_, d=ident:
            e.transpose(out=o, in_=i, identity=d), inc=inc)

    def act(out, in_, func, bias, scale=1.0, accum=None):
        P.r("s", lambda e, o=out, i=in_, f=func, b=bias, sc=scale, a=accum:
            e.activation(out=o, in_=i, func=f, bias=b, scale=sc, accum_out=a),
            inc="S")

    def vts(out, in0, s1, op0, s2=None, op1=None, eng="v"):
        inc = {"v": "V", "g": "G"}[eng]
        if op1 is None:
            P.r(eng, lambda e, o=out, i=in0, a=s1, p0=op0:
                e.tensor_scalar(out=o, in0=i, scalar1=a, scalar2=None, op0=p0),
                inc=inc)
        else:
            P.r(eng, lambda e, o=out, i=in0, a=s1, b=s2, p0=op0, p1=op1:
                e.tensor_scalar(out=o, in0=i, scalar1=a, scalar2=b,
                                op0=p0, op1=p1), inc=inc)

    def vtt(out, in0, in1, op):
        P.r("v", lambda e, o=out, a=in0, b=in1, p=op:
            e.tensor_tensor(out=o, in0=a, in1=b, op=p), inc="V")

    def vstt(out, in0, scalar, in1, op0, op1):
        P.r("v", lambda e, o=out, a=in0, sc=scalar, b=in1, p0=op0, p1=op1:
            e.scalar_tensor_tensor(out=o, in0=a, scalar=sc, in1=b,
                                   op0=p0, op1=p1), inc="V")

    def vcopy(out, in_, eng="v"):
        P.r(eng, lambda e, o=out, i=in_: e.tensor_copy(out=o, in_=i),
            inc={"v": "V", "g": "G"}[eng])

    def vred(out, in_, op):
        P.r("v", lambda e, o=out, i=in_, p=op:
            e.tensor_reduce(out=o, in_=i, axis=AX.X, op=p), inc="V")

    def vrecip(out, in_):
        P.r("v", lambda e, o=out, i=in_: e.reciprocal(out=o, in_=i), inc="V")

    def dma(eng, out, in_, sem=None):
        P.r(eng, lambda e, o=out, i=in_: e.dma_start(out=o, in_=i),
            inc=(sem if sem is not None else {"y": "Y", "g": "GD"}[eng]),
            k=16)

    def memset(ap, val, eng="v"):
        P.r(eng, lambda e, a=ap, v=val: e.memset(a, v),
            inc={"v": "V", "g": "G"}[eng])

    def fma_tile_view(buf, t):
        return buf[:, :].rearrange("p (c r) -> p c r", r=2048)[:, :, t * 128:(t + 1) * 128]

    def stt_c(c):
        return STT[:, c:c + 1]

    def vsettle():
        P.r("v", lambda e: e.tensor_copy(out=SCR[:, 3104:3136],
                                         in_=SCR[:, 3104:3136]), inc="V")

    def ssettle():
        P.r("s", lambda e: e.activation(out=SCR[:, 3104:3136],
                                        in_=SCR[:, 3104:3136],
                                        func=AF.Identity, bias=0.0,
                                        scale=1.0), inc="S")

    def ln_tile(ins, outs, dump_off=ODMP):
        n = len(ins)
        for i, (ap, wd) in enumerate(ins):
            vtt(SCR[:, dump_off:dump_off + wd], ap, ap, ALU.mult)
            vred(stt_c(CSQ0 + i), SCR[:, dump_off:dump_off + wd], ALU.add)
            vred(stt_c(CSU0 + i), ap, ALU.add)
        vsettle()
        vsettle()
        if n == 2:
            vtt(stt_c(CSU0), stt_c(CSU0), stt_c(CSU1), ALU.add)
            vtt(stt_c(CSQ0), stt_c(CSQ0), stt_c(CSQ1), ALU.add)
        vts(stt_c(CMEAN), stt_c(CSU0), 1.0 / D, ALU.mult)
        vsettle()
        vts(stt_c(CMSQE), stt_c(CMEAN), stt_c(CMEAN), ALU.mult,
            -EPS, ALU.subtract)
        vsettle()
        vts(stt_c(CVARP), stt_c(CSQ0), 1.0 / D, ALU.mult,
            stt_c(CMSQE), ALU.subtract)
        vsettle()
        vsettle()
        P.w("s", "V")
        act(stt_c(CSD), stt_c(CVARP), AF.Sqrt, zero)
        ssettle()
        P.w("v", "S")
        vrecip(stt_c(CRSTD), stt_c(CSD))
        vsettle()
        vts(stt_c(CNMUR), stt_c(CMEAN), stt_c(CRSTD), ALU.mult, -1.0, ALU.mult)
        vsettle()
        vsettle()
        P.w("s", "V")
        for (ap, wd), oap in zip(ins, outs):
            act(oap, ap, AF.Identity, stt_c(CNMUR), scale=stt_c(CRSTD))

    wl_state = {"half": 0, "mark": [[], []]}

    def land(src_rear3, nchunks, ncols):
        h = wl_state["half"]
        for sem, val in wl_state["mark"][h]:
            P.w("y", sem, val)
        wl_state["mark"][h] = []
        wl = WL[:, h * 1024:h * 1024 + nchunks * ncols]
        dma("y", wl.rearrange("p (c f) -> p c f", c=nchunks), src_rear3,
            sem="Y%d" % h)
        wl_state["half"] = 1 - h
        return h, P.n["Y%d" % h]

    def wl_mark(h, sem):
        wl_state["mark"][h].append((sem, P.n[sem]))

    def wrear(w_ap):
        return w_ap.rearrange("(c p) f -> p c f", p=128)

    slot_mark = {}

    def wait_slot(eng, key):
        if key in slot_mark:
            P.w(eng, "T", slot_mark[key])

    def set_slot(key):
        slot_mark[key] = P.n["T"]

    def load_w_dd(w_ap, dstbuf, dstkey, scale_coff=None, const_scale=None,
                  bias_rhs_coff=None, bias_dst_coff=None):
        for ci in range(DC):
            h, ydone = land(wrear(w_ap)[:, ci:ci + 1, :], 1, 1024)
            seg = WL[:, h * 1024:(h + 1) * 1024]
            od = dstbuf[:, ci * 1024:(ci + 1) * 1024]
            P.w("v", "Y%d" % h, ydone)
            if ci == 0:
                wait_slot("v", dstkey)
            if scale_coff is not None:
                vts(od, seg, COLS[:, scale_coff + ci:scale_coff + ci + 1],
                    ALU.mult)
            elif const_scale is not None:
                vts(od, seg, const_scale, ALU.mult)
            else:
                vcopy(od, seg)
            if bias_rhs_coff is not None:
                P.w("t", "Y%d" % h, ydone)
                pp = pX if ci % 2 == 0 else pE
                P.w("t", "V")
                for oc in range(8):
                    mm(pp[:, oc * 8:(oc + 1) * 8],
                       seg[:, oc * 128:(oc + 1) * 128],
                       COLS[:, bias_rhs_coff:bias_rhs_coff + 8],
                       start=True, stop=True, inc=("T" if oc == 7 else None))
                wl_mark(h, "T")
                wl_mark(h, "V")
                P.w("v", "T")
                vstt(COLS[:, bias_dst_coff:bias_dst_coff + 8],
                     COLS[:, bias_dst_coff:bias_dst_coff + 8], 1.0,
                     pp[:, 0:64].rearrange("p (o j) -> p o j", j=8)[:, :, ci],
                     ALU.mult, ALU.add)
            else:
                wl_mark(h, "V")

    def emit_dumps(names):
        allb = {"dbg_fma": FMA, "dbg_fmb": FMB, "dbg_str": STR,
                "dbg_kvb": KVB, "dbg_kvp": KVP, "dbg_cols": COLS,
                "dbg_ss": SS, "dbg_scb": SCB, "dbg_scr": SCR,
                "dbg_wa": WA, "dbg_wb": WB}
        for eng, semn in [("y", "T"), ("y", "V"), ("y", "S"), ("y", "G"),
                          ("y", "Y0"), ("y", "Y1"), ("y", "YS"),
                          ("y", "YK")]:
            P.w(eng, semn)
        for nm in names:
            if nm == "dbg_cols":
                dma("y", dbg[nm][:, 0:120], COLS[:, 0:120])
            else:
                dma("y", dbg[nm], allb[nm][:, :])
        P.w("y", "Y")

    if stop_after is not None:
        memset(SCB[:, :], 0.0)
        memset(SCR[:, :], 0.0)
        memset(SS[:, :], 0.0)
        P.w("y", "V")
    dma("y", EYEB[:, :], eyeB_in, sem="YS")
    dma("y", ONE1[0:1, :], ones_in, sem="YS")
    dma("y", SCR[0:64, 0:128], w_in["proj"], sem="YS")
    dma("y", SCR[64:128, 0:128], w_in["proj"], sem="YS")
    dma("y", COLS[:, 0:88], cols_in, sem="YS")
    dma("y", SCR[0:1, 512:2560], rows_in, sem="YS")
    for c in range(DC):
        dma("y", SCR[:, 128 + c * 8:128 + (c + 1) * 8],
            w_in["gate"][c * 128:(c + 1) * 128, :], sem="YS")
    dma("y", STR[:, :].rearrange("p (t f) -> p t f", t=NT),
        x_in.rearrange("(t p) f -> p t f", p=128), sem="YS")
    setup_y = P.n["YS"]

    memset(SCR[:, 3104:3136], 0.0)
    P.w("v", "YS", setup_y)
    memset(COLS[:, CB1P:CB1P + 32], 0.0)
    memset(stt_c(CZERO), 0.0)
    memset(stt_c(CPIO2), PI / 2)
    vcopy(PRJ[0:64, :], SCR[0:64, 0:128])
    vcopy(PRJ[64:128, :], SCR[64:128, 0:128])
    vcopy(GTB[:, :], SCR[:, 128:192])
    vcopy(ROWS[0:1, :], SCR[0:1, 512:2560])

    def barrier_all():
        for eng in ("t", "v", "s", "g", "y"):
            for semn in ("T", "V", "S", "G", "Y", "GD"):
                P.w(eng, semn)

    P.w("s", "YS", setup_y)
    xnb = SCB[:, OXN:OXN + 1024]
    for t in range(NT):
        xt = STR[:, t * D:(t + 1) * D]
        P.w("s", "T")
        ln_tile([(xt, 1024)], [xnb])
        ssettle()
        P.w("t", "S")
        P.w("t", "V")
        for c in range(DC):
            tpose(pT[:, c * 128:(c + 1) * 128], xnb[:, c * 128:(c + 1) * 128],
                  EYEB[:, :], inc=("T" if c == DC - 1 else None))
        P.w("v", "T")
        vcopy(fma_tile_view(FMA, t),
              pT[:, 0:1024].rearrange("p (c r) -> p c r", c=DC))

    if stop_after == "A":
        emit_dumps(["dbg_fma", "dbg_str"])
        return _finish(nc, stk, P, sems)

    load_w_dd(w_in["wk"], WA, "WA", scale_coff=CG1,
              bias_rhs_coff=CL1B, bias_dst_coff=CBK)
    load_w_dd(w_in["wv"], WB, "WB", scale_coff=CG1,
              bias_rhs_coff=CL1B, bias_dst_coff=CBV)
    wkv_cast = P.n["V"]
    wkv_bias = P.n["V"]
    if stop_after == "W":
        P.w("y", "G")
        emit_dumps(["dbg_cols", "dbg_wa", "dbg_wb"])
        return _finish(nc, stk, P, sems)

    KP = SCB[:, OKP:OKP + 1024]
    KPv = KP.rearrange("p (c r) -> p c r", r=256)
    vfm = SCB[:, OVFM:OVFM + 512]
    VPb = SCB[:, OVP:OVP + 512]
    kAb = SCB[:, OQA:OQA + 512]
    kBb = SCB[:, OQB:OQB + 512]

    def fourier(src_psum, sink_cos, sink_sin):
        u = SCR[:, OU:OU + 512]
        v = SCR[:, OV:OV + 512]
        r = SCR[:, OR_:OR_ + 512]
        a = SCR[:, OA:OA + 512]
        P.w("v", "S")
        vts(u, src_psum, 1.0 / (16 * PI), ALU.mult, MAGIC, ALU.add)
        vts(v, u, MAGIC, ALU.subtract, -2 * PI, ALU.mult)
        vstt(r, src_psum, 0.125, v, ALU.mult, ALU.add)
        vsettle()
        P.w("s", "V")
        P.w("s", "T")
        act(a, r, AF.Abs, zero)
        act(sink_cos, a, AF.Sin, stt_c(CPIO2), scale=-1.0)
        act(sink_sin, r, AF.Sin, zero)
        ssettle()

    P.w("t", "V", max(wkv_cast, wkv_bias))
    bank_mark = {}
    it_b = 0
    for pr in range(HP):
        memset(KVA[:, :], 0.0)
        for tc in range(4):
            kbank = pA if it_b % 2 == 0 else pF
            vbank = pC if it_b % 2 == 0 else pE
            P.w("t", "S", bank_mark.get(("k", it_b % 2), 0))
            for c in range(DC):
                mm(kbank[:, :], WA[:, c * 1024 + pr * 128:c * 1024 + (pr + 1) * 128],
                   FMA[:, c * 2048 + tc * 512:c * 2048 + (tc + 1) * 512],
                   start=(c == 0), stop=(c == DC - 1),
                   inc=("T" if c == DC - 1 else None))
            P.w("t", "S", bank_mark.get(("v", it_b % 2), 0))
            for c in range(DC):
                mm(vbank[:, :], WB[:, c * 1024 + pr * 128:c * 1024 + (pr + 1) * 128],
                   FMA[:, c * 2048 + tc * 512:c * 2048 + (tc + 1) * 512],
                   start=(c == 0), stop=(c == DC - 1),
                   inc=("T" if c == DC - 1 else None))
            k_mm = P.n["T"] - 1
            P.w("s", "T", k_mm)
            act(kAb[0:64, :], kbank[0:64, :], AF.Identity,
                COLS[0:64, CBK + pr:CBK + pr + 1])
            act(kBb[64:128, :], kbank[64:128, :], AF.Identity,
                COLS[64:128, CBK + pr:CBK + pr + 1])
            ssettle()
            k_evac = P.n["S"]
            bank_mark[("k", it_b % 2)] = k_evac
            P.w("s", "T")
            act(vfm[:, :], vbank[:, :], AF.Identity,
                COLS[:, CBV + pr:CBV + pr + 1])
            ssettle()
            v_evac = P.n["S"]
            bank_mark[("v", it_b % 2)] = v_evac
            it_b += 1
            P.w("t", "S", v_evac)
            P.w("t", "V")
            for ttc in range(4):
                tpose(pT[:, ttc * 128:(ttc + 1) * 128],
                      vfm[:, ttc * 128:(ttc + 1) * 128], EYEB[:, :],
                      inc=("T" if ttc == 3 else None))
            P.w("v", "T")
            vcopy(VPb[:, :], pT[:, 0:512])
            vsettle()
            vp_ready = P.n["V"]
            for hh in range(2):
                P.w("t", "S", k_evac)
                P.w("t", "V")
                for ttc in range(4):
                    if hh == 0:
                        mm(pB[:, ttc * 128:(ttc + 1) * 128],
                           kAb[0:64, ttc * 128:(ttc + 1) * 128],
                           PRJ[0:64, :], start=(ttc == 0), stop=(ttc == 3),
                           inc=("T" if ttc == 3 else None))
                    else:
                        mm(pB[:, ttc * 128:(ttc + 1) * 128],
                           kBb[64:128, ttc * 128:(ttc + 1) * 128],
                           PRJ[64:128, :], start=(ttc == 0), stop=(ttc == 3),
                           inc=("T" if ttc == 3 else None))
                P.w("v", "T")
                fourier(pB[:, :], KPv[:, :, 0:128], KPv[:, :, 128:256])
                kp_ready = P.n["S"]
                P.w("t", "S", kp_ready)
                P.w("t", "V", vp_ready)
                for ttc in range(4):
                    lh = VPb[:, ttc * 128 + hh * 64:ttc * 128 + hh * 64 + 64]
                    if hh == 0:
                        mm(pD[0:64, 0:256], lh, KP[:, ttc * 256:(ttc + 1) * 256],
                           start=(ttc == 0), stop=(ttc == 3),
                           inc=("T" if ttc == 3 else None))
                    else:
                        mm(pD[64:128, 0:256], lh, KP[:, ttc * 256:(ttc + 1) * 256],
                           start=(ttc == 0), stop=(ttc == 3),
                           tile_position=(0, 64),
                           inc=("T" if ttc == 3 else None))
            P.w("v", "T")
            vtt(KVA[:, :], pD[:, 0:256], KVA[:, :], ALU.add)
            P.w("t", "V")
        vsettle()
        vts(KVP[:, pr * 256:(pr + 1) * 256], KVA[:, :], 1.0, ALU.mult)
        if stop_after == "B0" and pr == 0:
            emit_dumps(["dbg_kvp", "dbg_scb", "dbg_scr"])
            return _finish(nc, stk, P, sems)
    kvp_done = P.n["V"]
    set_slot("WA")
    set_slot("WB")
    if stop_after == "B1":
        emit_dumps(["dbg_kvp", "dbg_fma"])
        return _finish(nc, stk, P, sems)

    P.w("g", "V", kvp_done)
    dma("g", cc_in_h.ap(), KVP[:, :])
    P.w("g", "GD")
    P.r("g", lambda e: e.collective_compute(
        "AllReduce", ALU.add,
        replica_groups=[[0, 1, 2, 3], [4, 5, 6, 7]],
        ins=[cc_in_h.ap().opt()], outs=[cc_out_h.ap().opt()]), inc="CC")

    load_w_dd(w_in["wq"], WA, "WA", scale_coff=CG1,
              bias_rhs_coff=CL1B, bias_dst_coff=CBQ)
    load_w_dd(w_in["wo"], WB, "WB", const_scale=1.0 / 128.0)
    wqo_cast = P.n["V"]
    wq_bias = P.n["V"]

    P.w("y", "CC")
    dma("y", KVP[0:64, :], cc_out_h.ap()[0:64, :], sem="YK")
    dma("y", SCB[0:64, 0:2048], cc_out_h.ap()[64:128, :], sem="YK")
    kvsum_y = P.n["YK"]

    P.w("t", "YK", kvsum_y)
    for half in range(4):
        P.w("t", "V")
        for i in range(8):
            gidx = half * 8 + i
            h = gidx // 2
            mc = gidx % 2
            pr = h // 2
            hh = h % 2
            buf = KVP if hh == 0 else SCB
            src = buf[0:64,
                      pr * 256 + mc * 128:pr * 256 + (mc + 1) * 128]
            idn = EYEB[0:64, 0:64]
            tpose(pT[:, i * 64:(i + 1) * 64], src, idn,
                  inc=("T" if i == 7 else None))
        P.w("v", "T")
        vcopy(KVB[:, half * 512:(half + 1) * 512], pT[:, 0:512])
    kvb_done = P.n["V"]

    if stop_after == "B":
        emit_dumps(["dbg_fma", "dbg_str", "dbg_kvb", "dbg_kvp", "dbg_cols"])
        return _finish(nc, stk, P, sems)

    ATT = SCB[:, OATT:OATT + 4096]
    QPC = SCB[:, OQPC:OQPC + 512]
    QPS = SCB[:, OQPS:OQPS + 512]
    qAb = SCB[:, OQA:OQA + 512]
    qBb = SCB[:, OQB:OQB + 512]
    xn2 = SCB[:, OXN:OXN + 1024]

    P.w("t", "V", max(wqo_cast, wq_bias, kvb_done))
    qbank_mark = {}
    it_c = 0
    for tc in range(4):
        P.w("v", "S")
        for pr in range(HP):
            qbank = pA if it_c % 2 == 0 else pF
            P.w("t", "S", qbank_mark.get(it_c % 2, 0))
            for c in range(DC):
                mm(qbank[:, :], WA[:, c * 1024 + pr * 128:c * 1024 + (pr + 1) * 128],
                   FMA[:, c * 2048 + tc * 512:c * 2048 + (tc + 1) * 512],
                   start=(c == 0), stop=(c == DC - 1),
                   inc=("T" if c == DC - 1 else None))
            P.w("s", "T")
            act(qAb[0:64, :], qbank[0:64, :], AF.Identity,
                COLS[0:64, CBQ + pr:CBQ + pr + 1])
            act(qBb[64:128, :], qbank[64:128, :], AF.Identity,
                COLS[64:128, CBQ + pr:CBQ + pr + 1])
            ssettle()
            q_evac = P.n["S"]
            qbank_mark[it_c % 2] = q_evac
            it_c += 1
            for hh in range(2):
                hgl = 2 * pr + hh
                P.w("t", "S", q_evac)
                P.w("t", "V")
                if hh == 0:
                    mm(pB[:, :], PRJ[0:64, :], qAb[0:64, :],
                       start=True, stop=True, inc="T")
                else:
                    mm(pB[:, :], PRJ[64:128, :], qBb[64:128, :],
                       start=True, stop=True, inc="T")
                P.w("v", "T")
                if stop_after == "C1" and tc == 0 and pr == 0 and hh == 0:
                    vcopy(SCR[:, 2048:2560], pB[:, :])
                    vcopy(SCR[:, 2560:2688], PRJ[:, :])
                    vcopy(SCR[:, 2688:2816], pA[:, 0:128])
                    emit_dumps(["dbg_scb", "dbg_str", "dbg_fma", "dbg_scr", "dbg_wa", "dbg_wb"])
                    return _finish(nc, stk, P, sems)
                fourier(pB[:, :], QPC, QPS)
                qp_ready = P.n["S"]
                P.w("t", "S", qp_ready)
                if hh == 0:
                    mm(pC[0:64, :], KVB[:, hgl * 128:hgl * 128 + 64], QPC,
                       start=True, stop=False)
                    mm(pC[0:64, :], KVB[:, hgl * 128 + 64:hgl * 128 + 128], QPS,
                       start=False, stop=True, inc="T")
                else:
                    mm(pC[64:128, :], KVB[:, hgl * 128:hgl * 128 + 64], QPC,
                       start=True, stop=False, tile_position=(0, 64))
                    mm(pC[64:128, :], KVB[:, hgl * 128 + 64:hgl * 128 + 128], QPS,
                       start=False, stop=True, tile_position=(0, 64), inc="T")
            P.w("s", "T")
            act(ATT[:, pr * 512:(pr + 1) * 512], pC[:, :], AF.Identity, zero)
            ssettle()
        att_ready = P.n["S"]
        for tsub in range(4):
            t = tc * 4 + tsub
            P.w("t", "S", att_ready)
            P.w("t", "V")
            for nh in range(2):
                pp = pD if nh == 0 else pE
                mm(pp[:, :], ONE1[0:1, :], ROWS[0:1, nh * 512:(nh + 1) * 512],
                   start=True, stop=False)
                for pr in range(HP):
                    mm(pp[:, :],
                       ATT[:, pr * 512 + tsub * 128:pr * 512 + (tsub + 1) * 128],
                       WB[:, pr * 1024 + nh * 512:pr * 1024 + (nh + 1) * 512],
                       start=False, stop=(pr == HP - 1),
                       inc=("T" if pr == HP - 1 else None))
            P.w("v", "T")
            Asc = SCR[:, OASC:OASC + 1024]
            vtt(Asc[:, 0:512], pD[:, :], STR[:, t * D:t * D + 512], ALU.add)
            vtt(Asc[:, 512:1024], pE[:, :], STR[:, t * D + 512:(t + 1) * D],
                ALU.add)
            vtt(STR[:, t * D:(t + 1) * D], Asc[:, :],
                STR[:, t * D:(t + 1) * D], ALU.add)
            P.w("s", "V")
            P.w("s", "T")
            ln_tile([(Asc, 1024)], [xn2])
            ssettle()
            P.w("t", "S")
            P.w("t", "V")
            for c in range(DC):
                tpose(pT[:, c * 128:(c + 1) * 128],
                      xn2[:, c * 128:(c + 1) * 128], EYEB[:, :],
                      inc=("T" if c == DC - 1 else None))
            P.w("v", "T")
            vcopy(fma_tile_view(FMB, t),
                  pT[:, 0:1024].rearrange("p (c r) -> p c r", c=DC))

    set_slot("WA")
    set_slot("WB")

    if stop_after == "C":
        emit_dumps(["dbg_fma", "dbg_fmb", "dbg_str", "dbg_kvb", "dbg_cols",
                    "dbg_scb", "dbg_wa", "dbg_wb"])
        return _finish(nc, stk, P, sems)

    P.w("v", "T")
    vcopy(COLS[:, CB1P:CB1P + 32], COLS[:, CB1:CB1 + 32])
    HB = FMA
    outb_mark = {}
    for hs in range(DC):
        slot = hs % 2
        wslot = WA[:, slot * 4096:(slot + 1) * 4096]
        for lnd in range(4):
            h, ydone = land(
                wrear(w_in["w1"])[:, lnd * 2:lnd * 2 + 2,
                                  hs * 512:(hs + 1) * 512], 2, 512)
            P.w("v", "Y%d" % h, ydone)
            if lnd == 0:
                wait_slot("v", ("W1", slot))
                if hs < 2:
                    wait_slot("v", "WA")
            for j in range(2):
                ci = lnd * 2 + j
                seg = WL[:, h * 1024 + j * 512:h * 1024 + (j + 1) * 512]
                vts(wslot[:, ci * 512:(ci + 1) * 512], seg,
                    COLS[:, CG2 + ci:CG2 + ci + 1], ALU.mult)
            P.w("t", "Y%d" % h, ydone)
            for j in range(2):
                ci = lnd * 2 + j
                seg = WL[:, h * 1024 + j * 512:h * 1024 + (j + 1) * 512]
                P.w("t", "V")
                for hc in range(4):
                    mm(pX[:, hc * 8:(hc + 1) * 8],
                       seg[:, hc * 128:(hc + 1) * 128],
                       COLS[:, CL2B:CL2B + 8],
                       start=True, stop=True,
                       inc=("T" if hc == 3 else None))
                P.w("v", "T")
                vstt(COLS[:, CB1P + hs * 4:CB1P + hs * 4 + 4],
                     COLS[:, CB1P + hs * 4:CB1P + hs * 4 + 4], 1.0,
                     pX[:, 0:32].rearrange(
                         "p (o j) -> p o j", j=8)[:, :, ci],
                     ALU.mult, ALU.add)
            wl_mark(h, "T")
            wl_mark(h, "V")
        w1_ready = P.n["V"]
        b1p_done = P.n["V"]
        w2slot = WB[:, slot * 4096:(slot + 1) * 4096]
        for lnd in range(4):
            h, ydone = land(
                wrear(w_in["w2"])[:, hs * 4 + lnd:hs * 4 + lnd + 1, :], 1, 1024)
            P.w("v", "Y%d" % h, ydone)
            if lnd == 0:
                wait_slot("v", ("W2", slot))
                if hs < 2:
                    wait_slot("v", "WB")
            seg = WL[:, h * 1024:(h + 1) * 1024]
            vcopy(w2slot[:, lnd * 1024:(lnd + 1) * 1024], seg)
            wl_mark(h, "V")
        w2_ready = P.n["V"]
        hbase = slot * 8192
        P.w("t", "V", w1_ready)
        for tq in range(4):
            for hc in range(4):
                hcg = hs * 4 + hc
                par = (tq * 4 + hc) % 2
                pp = pA if par == 0 else pB
                P.w("t", "S", bank_mark.get(("dh", par), 0))
                for c in range(DC):
                    mm(pp[:, :],
                       wslot[:, c * 512 + hc * 128:c * 512 + (hc + 1) * 128],
                       FMB[:, c * 2048 + tq * 512:c * 2048 + (tq + 1) * 512],
                       start=(c == 0), stop=(c == DC - 1),
                       inc=("T" if c == DC - 1 else None))
                P.w("s", "T")
                P.w("s", "V", b1p_done)
                act(HB[:, hbase + (tq * 4 + hc) * 512:
                       hbase + (tq * 4 + hc + 1) * 512],
                    pp[:, :], AF.Relu,
                    COLS[:, CB1P + hcg:CB1P + hcg + 1])
                bank_mark[("dh", par)] = P.n["S"]
        set_slot(("W1", slot))
        ssettle()
        h_done_s = P.n["S"]
        P.w("t", "V", w2_ready)
        for w in range(16):
            tq, tsub = w // 4, w % 4
            t = tq * 4 + tsub
            for nh in range(2):
                g = w * 2 + nh
                bank = [pC, pD, pE, pF][g % 4]
                P.w("t", "V", outb_mark.get(g % 4, 0))
                if w == 0 and nh == 0:
                    P.w("t", "S", h_done_s)
                if hs == 0:
                    mm(bank[:, :], ONE1[0:1, :],
                       ROWS[0:1, 1024 + nh * 512:1024 + (nh + 1) * 512],
                       start=True, stop=False)
                for hc in range(4):
                    hcol = hbase + (tq * 4 + hc) * 512 + tsub * 128
                    mm(bank[:, :], HB[:, hcol:hcol + 128],
                       w2slot[:, hc * 1024 + nh * 512:hc * 1024 + (nh + 1) * 512],
                       start=(hs != 0 and hc == 0), stop=(hc == 3),
                       inc=("T" if hc == 3 else None))
                P.w("v", "T")
                vtt(STR[:, t * D + nh * 512:t * D + (nh + 1) * 512],
                    bank[:, :],
                    STR[:, t * D + nh * 512:t * D + (nh + 1) * 512], ALU.add)
                outb_mark[g % 4] = P.n["V"]
        set_slot(("W2", slot))
    set_slot("FMB")

    for t in range(NT):
        png = t % 2
        P.w("v", "S")
        vcopy(SCR[:, png * 1024:(png + 1) * 1024], STR[:, t * D:(t + 1) * D])
        ln_tile([(SCR[:, png * 1024:(png + 1) * 1024], 1024)],
                [STR[:, t * D:(t + 1) * D]])
        ssettle()

    if stop_after == "D":
        emit_dumps(["dbg_str", "dbg_cols"])
        return _finish(nc, stk, P, sems)

    ybs = SCB[:, OXN:OXN + 1024]
    for t in range(NT):
        P.w("v", "S")
        P.w("v", "T")
        vts(ybs[:, :], STR[:, t * D:(t + 1) * D], 1.0, ALU.mult)
        P.w("t", "V")
        for c in range(DC):
            tpose(pT[:, c * 128:(c + 1) * 128], ybs[:, c * 128:(c + 1) * 128],
                  EYEB[:, :], inc=("T" if c == DC - 1 else None))
        P.w("v", "T")
        vcopy(fma_tile_view(FMA, t),
              pT[:, 0:1024].rearrange("p (c r) -> p c r", c=DC))
    vsettle()
    y1t_done = P.n["V"]

    P.w("t", "V", y1t_done)
    for tt in range(NT):
        P.w("t", "S")
        for c in range(DC):
            mm(pX[:, 0:8], FMA[:, c * 2048 + tt * 128:c * 2048 + (tt + 1) * 128],
               GTB[:, c * 8:(c + 1) * 8], start=(c == 0), stop=(c == DC - 1),
               inc=("T" if c == DC - 1 else None))
        P.w("v", "T")
        ssl = SS[:, tt * 8:(tt + 1) * 8]
        vred(stt_c(CG_A), pX[:, 0:8], ALU.max)
        vsettle()
        vts(stt_c(CG_B), stt_c(CG_A), -1.0, ALU.mult)
        vsettle()
        P.w("s", "V")
        act(SCR[:, OG1:OG1 + 8], pX[:, 0:8], AF.Exp, stt_c(CG_B))
        ssettle()
        P.w("v", "S")
        vred(stt_c(CG_A), SCR[:, OG1:OG1 + 8], ALU.add)
        vsettle()
        vrecip(stt_c(CG_B), stt_c(CG_A))
        vsettle()
        vts(ssl, SCR[:, OG1:OG1 + 8], stt_c(CG_B), ALU.mult)
        vsettle()
        vred(stt_c(CG_C), ssl, ALU.max)
        vsettle()
        vts(SCR[:, OG2:OG2 + 8], ssl, stt_c(CG_C), ALU.is_ge)
        vsettle()
        vstt(SCR[:, OG3:OG3 + 8], SCR[:, OG2:OG2 + 8], -1e30, ssl,
             ALU.mult, ALU.add)
        vsettle()
        vred(stt_c(CG_D), SCR[:, OG3:OG3 + 8], ALU.max)
        vsettle()
        vts(SCR[:, OG2:OG2 + 8], ssl, stt_c(CG_D), ALU.is_ge)
        vsettle()
        vtt(ssl, ssl, SCR[:, OG2:OG2 + 8], ALU.mult)
    gate_done = P.n["V"]

    h1b = [SCB[:, OH1:OH1 + 512], SCB[:, 2048 + OH1:2048 + OH1 + 512]]
    a3b = [SCB[:, OA3:OA3 + 512], SCB[:, 2048 + OA3:2048 + OA3 + 512]]
    hsb = [SCB[:, OHS:OHS + 512], SCB[:, 2048 + OHS:2048 + OHS + 512]]
    hfb = [SCB[:, OHF:OHF + 512], SCB[:, 2048 + OHF:2048 + OHF + 512]]
    P.w("s", "V", gate_done)
    a3_mark = {0: 0, 1: 0}
    hf_mark = {0: 0, 1: 0}
    tp_mark = {0: 0, 1: 0}
    last_silu = [0]
    last_a3 = [0]
    pending = None

    def emit_tail(nxt_raws_done):
        par, grp, tq, ei, tsub, hsb_v, _ = pending
        gp = grp % 2
        e = grp * 2 + ei
        P.w("t", "V", hsb_v)
        for i4 in range(4):
            tpose(pT[:, par * 512 + i4 * 128:par * 512 + (i4 + 1) * 128],
                  hsb[par][:, i4 * 128:(i4 + 1) * 128], EYEB[:, :],
                  inc=("T" if i4 == 3 else None))
        tp_mark[par] = P.n["T"]
        P.w("v", "T", tp_mark[par])
        vcopy(hfb[par], pT[:, par * 512:par * 512 + 512])
        vsettle()
        hf_mark[par] = P.n["V"]
        P.w("t", "V", hf_mark[par])
        for nh in range(2):
            pp = [pC, pD, pE, pF][tsub * 2 + nh]
            for hc in range(4):
                w2b = gp * 8192 + ei * 4096 + hc * 1024 + nh * 512
                mm(pp[:, :], hfb[par][:, hc * 128:(hc + 1) * 128],
                   FMB[:, w2b:w2b + 512],
                   start=(ei == 0 and hc == 0),
                   stop=(ei == 1 and hc == 3),
                   inc=("T" if hc == 3 else None))
        return P.n["T"]

    evac_pending = []

    def emit_evac(grp_tq):
        tq = grp_tq
        for tsub in range(2):
            tt = tq * 2 + tsub
            for nh in range(2):
                pp = [pC, pD, pE, pF][tsub * 2 + nh]
                P.w("v", "T")
                vtt(STR[:, tt * D + nh * 512:tt * D + (nh + 1) * 512],
                    pp[:, :],
                    STR[:, tt * D + nh * 512:tt * D + (nh + 1) * 512],
                    ALU.add)

    itidx = 0
    for grp in range(4):
        gp = grp % 2
        for ei in range(2):
            e = grp * 2 + ei
            for src_nm, dbuf, base, key in [
                    ("mw1", WA, ei * 4096, ("EW1", ei)),
                    ("mw3", WB, ei * 4096, ("EW3", ei))]:
                wsl = w_in[src_nm][e * D:(e + 1) * D, :]
                for lnd in range(4):
                    h, ydone = land(wrear(wsl)[:, lnd * 2:lnd * 2 + 2, :], 2, 512)
                    P.w("v", "Y%d" % h, ydone)
                    if lnd == 0:
                        wait_slot("v", key)
                        if src_nm == "mw1":
                            wait_slot("v", ("W1", ei))
                        else:
                            wait_slot("v", ("W2", 0))
                            wait_slot("v", ("W2", 1))
                            wait_slot("v", "WB")
                    for j in range(2):
                        ci = lnd * 2 + j
                        seg = WL[:, h * 1024 + j * 512:h * 1024 + (j + 1) * 512]
                        vcopy(dbuf[:, base + ci * 512:base + (ci + 1) * 512],
                              seg)
                    wl_mark(h, "V")
            w2sl = w_in["mw2"][e * HE:(e + 1) * HE, :]
            fbase = gp * 8192 + ei * 4096
            for lnd in range(4):
                h, ydone = land(wrear(w2sl)[:, lnd:lnd + 1, :], 1, 1024)
                P.w("v", "Y%d" % h, ydone)
                if lnd == 0:
                    wait_slot("v", ("EW2", gp, ei))
                    wait_slot("v", "FMB")
                seg = WL[:, h * 1024:(h + 1) * 1024]
                vcopy(FMB[:, fbase + lnd * 1024:fbase + (lnd + 1) * 1024],
                      seg)
                wl_mark(h, "V")
        ew_cast = P.n["V"]
        P.w("t", "V", ew_cast)
        for tq in range(8):
            for ei in range(2):
                e = grp * 2 + ei
                for tsub in range(2):
                    par = itidx % 2
                    tt = tq * 2 + tsub
                    toff = tq * 256 + tsub * 128
                    P.w("t", "S", last_silu[0])
                    P.w("t", "V", last_a3[0])
                    for c in range(DC):
                        mm(pA[:, :], FMA[:, c * 2048 + toff:c * 2048 + toff + 128],
                           WA[:, ei * 4096 + c * 512:ei * 4096 + (c + 1) * 512],
                           start=(c == 0), stop=(c == DC - 1),
                           inc=("T" if c == DC - 1 else None))
                    for c in range(DC):
                        mm(pB[:, :], FMA[:, c * 2048 + toff:c * 2048 + toff + 128],
                           WB[:, ei * 4096 + c * 512:ei * 4096 + (c + 1) * 512],
                           start=(c == 0), stop=(c == DC - 1),
                           inc=("T" if c == DC - 1 else None))
                    raws_done = P.n["T"]
                    P.w("s", "T", raws_done)
                    P.w("s", "V", hf_mark[par])
                    sse = SS[:, tt * 8 + e:tt * 8 + e + 1]
                    act(h1b[par], pA[:, :], AF.Silu, zero, scale=sse)
                    silu_s = P.n["S"]
                    last_silu[0] = silu_s
                    P.w("v", "S", silu_s)
                    P.w("v", "T", tp_mark[par])
                    vts(a3b[par], pB[:, :], sse, ALU.mult)
                    last_a3[0] = P.n["V"]
                    vtt(hsb[par], h1b[par], a3b[par], ALU.mult)
                    vsettle()
                    a3_mark[par] = P.n["V"]
                    hsb_v = P.n["V"]
                    prev = pending
                    if prev is not None:
                        emit_tail(raws_done)
                        if prev[3] == 1 and prev[4] == 1:
                            emit_evac(prev[2])
                    pending = (par, grp, tq, ei, tsub, hsb_v, silu_s)
                    itidx += 1
        for ei2 in range(2):
            set_slot(("EW1", ei2))
            set_slot(("EW3", ei2))
            set_slot(("EW2", gp, ei2))
    if pending is not None:
        emit_tail(P.n["T"])
        emit_evac(pending[2])
        pending = None
    moe_done = P.n["V"]

    out_mark = [0, 0]
    P.w("s", "V", moe_done)
    for t in range(NT):
        png = t % 2
        if out_mark[png] > 0:
            P.w("s", "YO%d" % png, out_mark[png])
        ln_tile([(STR[:, t * D:(t + 1) * D], 1024)],
                [SCR[:, png * 1024:(png + 1) * 1024]])
        ssettle()
        s_done = P.n["S"]
        P.w("y", "S", s_done)
        dma("y", y_out[t * 128:(t + 1) * 128, :],
            SCR[:, png * 1024:(png + 1) * 1024], sem="YO%d" % png)
        out_mark[png] = P.n["YO%d" % png]

    return _finish(nc, stk, P, sems)


def _finish(nc, stk, P, sems):
    for semn in ("Y", "Y0", "Y1", "YS", "YK", "YO0", "YO1"):
        P.w("y", semn)
    block = stk.enter_context(nc.Block())

    def replay(engname):
        def run(e):
            for step in P.q[engname]:
                if step[0] == "w":
                    e.wait_ge(sems[step[1]], step[2])
                else:
                    instr = step[1](e)
                    if step[2] is not None:
                        instr.then_inc(sems[step[2]], step[3])
        return run

    block.tensor(replay("t"))
    block.vector(replay("v"))
    block.scalar(replay("s"))
    block.gpsimd(replay("g"))
    block.sync(replay("y"))
    stk.close()
    return nc


def make_inputs(inputs):
    f32 = lambda a: np.ascontiguousarray(np.asarray(a), dtype=np.float32)
    cols = np.concatenate([
        f32(inputs["L0_attn_ln1_g"]).reshape(8, 128).T,
        f32(inputs["L0_attn_ln1_b"]).reshape(8, 128).T,
        f32(inputs["L0_attn_ln2_g"]).reshape(8, 128).T,
        f32(inputs["L0_attn_ln2_b"]).reshape(8, 128).T,
        f32(inputs["L0_attn_b1"]).reshape(32, 128).T,
        f32(inputs["L0_attn_bq"]).reshape(8, 128).T,
        f32(inputs["L0_attn_bk"]).reshape(8, 128).T,
        f32(inputs["L0_attn_bv"]).reshape(8, 128).T,
    ], axis=1)
    rows = np.concatenate([
        f32(inputs["L0_attn_bo"]).reshape(1, 1024),
        f32(inputs["L0_attn_b2"]).reshape(1, 1024),
    ], axis=1)
    base = {
        "wq": f32(inputs["L0_attn_wq"]),
        "wk": f32(inputs["L0_attn_wk"]),
        "wv": f32(inputs["L0_attn_wv"]),
        "wo": f32(inputs["L0_attn_wo"]),
        "w1": f32(inputs["L0_attn_w1"]),
        "w2": f32(inputs["L0_attn_w2"]),
        "proj": f32(inputs["L0_attn_proj_mat"]),
        "gate": f32(inputs["L0_ffn_gate_w"]),
        "mw1": f32(inputs["L0_ffn_w1"]).reshape(NE * D, HE),
        "mw3": f32(inputs["L0_ffn_w3"]).reshape(NE * D, HE),
        "mw2": f32(inputs["L0_ffn_w2"]).reshape(NE * HE, D),
        "colsin": np.ascontiguousarray(cols),
        "rowsin": np.ascontiguousarray(rows),
        "eyeB": np.eye(128, dtype=BFNP),
        "ones1": np.ones((1, 128), dtype=BFNP),
    }
    x = np.asarray(inputs["L0_attn_x"], dtype=np.float32).reshape(16384, D)
    shards = [np.ascontiguousarray(x[i * T:(i + 1) * T]) for i in range(8)]
    return base, shards


_NC_CACHE = {}


def kernel(**inputs):
    if "nc" not in _NC_CACHE:
        _NC_CACHE["nc"] = build_nc()
    nc = _NC_CACHE["nc"]
    base, shards = make_inputs(inputs)
    in_maps = [dict(base, x=shards[i]) for i in range(8)]
    res = run_bass_kernel_spmd(nc, in_maps, core_ids=list(range(8)))
    out = np.concatenate([np.asarray(res.results[c]["y"]) for c in range(8)],
                         axis=0)
    return out.reshape(2, 8192, 1024)
